# version 12
# baseline (speedup 1.0000x reference)
import math
import numpy as np
import ml_dtypes

import concourse.bass as bass
import concourse.mybir as mybir
from concourse.bass_utils import run_bass_kernel_spmd

F32 = mybir.dt.float32
BF16 = mybir.dt.bfloat16
I32 = mybir.dt.int32
AF = mybir.ActivationFunctionType
ALU = mybir.AluOpType
AX = mybir.AxisListType

NCORES = 8
D = 4096
NH = 32
HD = 128
HPC = NH // NCORES
E = 8
R = 16
NSLOT = 2 * E - 1
EPS = 1e-5
SCALING = 2.0

ENGS = ("sync", "tensor", "vector", "scalar", "gpsimd")


NDMA_SLOTS = 8


class _Op:
    __slots__ = ("eng", "fn", "kind", "deps", "gseq", "needed", "event",
                 "inc_amt", "clock_after", "tag", "qslot")

    def __init__(self, eng, fn, kind, gseq, tag):
        self.eng = eng
        self.fn = fn
        self.kind = kind
        self.deps = []
        self.gseq = gseq
        self.needed = False
        self.event = None
        self.inc_amt = 1
        self.clock_after = None
        self.tag = tag
        self.qslot = 0


def _ap_region(ap):
    try:
        t = ap.tensor
        name = t.name
        esz = mybir.dt.size(ap.dtype)
        pairs = [(int(s), int(c)) for s, c in ap.ap]
        off = int(ap.offset)
        if type(t).__name__ == "DRamTensorHandle":
            ext = sum(abs(s) * (c - 1) for s, c in pairs)
            return (name, off * esz, (off + ext + 1) * esz)
        rowlen_b = mybir.dt.size(t.dtype)
        for s in list(t.shape)[1:]:
            rowlen_b *= int(s)
        rowlen = rowlen_b // esz
        pidx = None
        for i, (s, c) in enumerate(pairs):
            if s == rowlen and c <= 128:
                pidx = i
                break
        if pidx is None:
            return (name, 0, rowlen_b)
        f_lo = (off % rowlen) * esz
        ext = sum(abs(s) * (c - 1) for i, (s, c) in enumerate(pairs) if i != pidx) * esz
        return (name, f_lo, f_lo + ext + esz)
    except Exception:
        try:
            return (ap.tensor.name, 0, 1 << 60)
        except Exception:
            return ("?", 0, 1 << 60)


class Em:
    def __init__(self):
        self.eng_ops = {e: [] for e in ENGS}
        self.gseq = 0
        self.wlog = {}
        self.rlog = {}
        self.frontier = {}
        self.dma_ring = {e: [None] * NDMA_SLOTS for e in ENGS}
        self.dma_ctr = {e: 0 for e in ENGS}
        self.last_cc = None

    def _dadd(self, deps, op):
        if op.kind == 'cc':
            key = ('cc',)
        elif op.kind == 'd':
            key = ('d', op.eng, op.qslot)
        else:
            key = ('c', op.eng)
        cur = deps.get(key)
        if cur is None or op.gseq > cur.gseq:
            deps[key] = op

    def op(self, eng, fn, r=(), w=(), kind='c', tag=""):
        o = _Op(eng, fn, kind, self.gseq, tag)
        self.gseq += 1
        deps = {}
        if kind == 'd':
            o.qslot = self.dma_ctr[eng] % NDMA_SLOTS
            self.dma_ctr[eng] += 1
            prev = self.dma_ring[eng][o.qslot]
            if prev is not None:
                self._dadd(deps, prev)
            self.dma_ring[eng][o.qslot] = o
        elif kind == 'cc':
            if self.last_cc is not None:
                self._dadd(deps, self.last_cc)
            self.last_cc = o
        regs_r = [_ap_region(a) for a in r]
        regs_w = [_ap_region(a) for a in w]
        for (n, lo, hi) in regs_r + regs_w:
            for (l2, h2, wop) in self.wlog.get(n, ()):
                if lo < h2 and l2 < hi and wop is not o:
                    self._dadd(deps, wop)
        for (n, lo, hi) in regs_w:
            for (l2, h2, rop) in self.rlog.get(n, ()):
                if lo < h2 and l2 < hi and rop is not o:
                    self._dadd(deps, rop)
        for fop in self.frontier.values():
            self._dadd(deps, fop)
        o.deps = list(deps.values())
        for d in o.deps:
            if d.kind == 'c' and kind == 'c' and d.eng == 'tensor' and eng == 'tensor':
                continue
            d.needed = True
        for (n, lo, hi) in regs_w:
            lst = self.wlog.setdefault(n, [])
            lst[:] = [e2 for e2 in lst if not (lo <= e2[0] and e2[1] <= hi)]
            lst.append((lo, hi, o))
            rl = self.rlog.get(n)
            if rl:
                rl[:] = [e2 for e2 in rl if not (lo <= e2[0] and e2[1] <= hi)]
        for (n, lo, hi) in regs_r:
            self.rlog.setdefault(n, []).append((lo, hi, o))
        self.eng_ops[eng].append(o)
        return o

    def barrier(self):
        f = {}
        for e in ENGS:
            seen = set()
            for o in reversed(self.eng_ops[e]):
                key = (o.kind, o.qslot) if o.kind == 'd' else (o.kind,)
                if key not in seen:
                    o.needed = True
                    f[(e,) + key] = o
                    seen.add(key)
                if len(seen) >= 2 + NDMA_SLOTS:
                    break
        self.frontier = f

    def finalize(self, nc):
        import contextlib
        self._es = contextlib.ExitStack()
        sems = {}
        counts = {}

        def get(name):
            if name not in sems:
                sems[name] = self._es.enter_context(nc.semaphore(name))
                counts[name] = 0
        for e in ENGS:
            for o in self.eng_ops[e]:
                if o.kind == 'd':
                    sn = f"sd_{e}_{o.qslot}"
                    get(sn)
                    counts[sn] += 16
                    o.event = (sn, counts[sn])
                    o.inc_amt = 16
                elif o.kind == 'cc':
                    sn = "scc"
                    get(sn)
                    counts[sn] += 1
                    o.event = (sn, counts[sn])
                    o.inc_amt = 1
                elif o.needed:
                    sn = f"se_{e}"
                    get(sn)
                    counts[sn] += 1
                    o.event = (sn, counts[sn])
                    o.inc_amt = 1
        fe = []
        for sn, cnt in counts.items():
            if (sn.startswith("sd_") or sn == "scc") and cnt > 0:
                fe.append((sn, cnt))
        for e in ENGS:
            for o in reversed(self.eng_ops[e]):
                if o.event is not None and o.kind == 'c':
                    fe.append(o.event)
                    break
        self.sems = sems
        self.final_events = fe

    def emit(self, eng_obj, ename):
        sems = self.sems
        clock = {}
        for o in self.eng_ops[ename]:
            waits = {}
            for d in o.deps:
                if (d.kind == 'c' and o.kind == 'c' and d.eng == 'tensor'
                        and ename == 'tensor'):
                    continue
                if d.event is None:
                    continue
                sn, v = d.event
                if clock.get(sn, 0) < v and waits.get(sn, (0, None))[0] < v:
                    waits[sn] = (v, d)
            merged = False
            for sn, (v, d) in waits.items():
                eng_obj.wait_ge(sems[sn], v)
                if clock.get(sn, 0) < v:
                    clock[sn] = v
                if d.clock_after:
                    merged = True
                    for k2, v2 in d.clock_after.items():
                        if clock.get(k2, 0) < v2:
                            clock[k2] = v2
            inst = o.fn(eng_obj)
            if o.event is not None:
                assert inst is not None, f"op {o.tag} has no instruction for its event"
                inst.then_inc(sems[o.event[0]], o.inc_amt)
            if waits or merged:
                o.clock_after = dict(clock)
            else:
                o.clock_after = clock.copy() if clock else None
        if ename == "sync":
            for sn, v in self.final_events:
                eng_obj.wait_ge(sems[sn], v)


def build_program(T):
    TPC = T // NCORES
    NG2 = T // 256
    NT128 = T // 128
    NQC = T // 512
    OT = TPC // 128
    NTH = TPC // 512
    assert TPC % 512 == 0

    nc = bass.Bass("TRN2", target_bir_lowering=False)

    x_full = nc.dram_tensor("x_full", [T, D], F32, kind="ExternalInput")
    x_own = nc.dram_tensor("x_own", [TPC, D], F32, kind="ExternalInput")
    wq_my = nc.dram_tensor("wq_my", [D, HPC * HD], F32, kind="ExternalInput")
    wk_my = nc.dram_tensor("wk_my", [D, HD], F32, kind="ExternalInput")
    wv_my = nc.dram_tensor("wv_my", [D, HD], F32, kind="ExternalInput")
    wo = nc.dram_tensor("wo", [32, 128, 32, 128], F32, kind="ExternalInput")
    bw_t = nc.dram_tensor("bw_t", [32, 128, 32, 128], F32, kind="ExternalInput")
    bb_t = nc.dram_tensor("bb_t", [128, 32], F32, kind="ExternalInput")
    rw_t = nc.dram_tensor("rw_t", [D, 16], F32, kind="ExternalInput")
    rb = nc.dram_tensor("rb", [NSLOT, 1], F32, kind="ExternalInput")
    a_t = nc.dram_tensor("a_t", [D, 128], F32, kind="ExternalInput")
    b_t = nc.dram_tensor("b_t", [128, D], F32, kind="ExternalInput")
    emap_i = nc.dram_tensor("emap_i", [NSLOT, 1], I32, kind="ExternalInput")
    tri_c = nc.dram_tensor("tri_c", [128, 4 * 512], BF16, kind="ExternalInput")
    id_in = nc.dram_tensor("id_in", [128, 128], BF16, kind="ExternalInput")
    iota_c = nc.dram_tensor("iota_c", [NSLOT, E], F32, kind="ExternalInput")
    expand_c = nc.dram_tensor("expand_c", [E, 128], BF16, kind="ExternalInput")
    goff_d = nc.dram_tensor("goff", [NCORES, 1], I32, kind="ExternalInput")

    out_d = nc.dram_tensor("out", [TPC, D], F32, kind="ExternalOutput")

    from concourse.replica_groups import maybe_share_collective_output_space
    RG = [list(range(NCORES))]
    sp = maybe_share_collective_output_space("AllGather", RG)
    stats_ctr = nc.dram_tensor("stats_ctr", [128, 16], F32, kind="Internal")
    stats_g = nc.dram_tensor("stats_g", [128 * NCORES, 16], F32, kind="Internal",
                             addr_space=sp)
    attnm_ctr = nc.dram_tensor("attnm_ctr", [T, HPC * HD], BF16, kind="Internal")
    attnm_g = nc.dram_tensor("attnm_g", [T * NCORES, HPC * HD], BF16, kind="Internal",
                             addr_space=sp)
    x2_d = nc.dram_tensor("x2_d", [TPC, D], F32, kind="Internal")

    em = Em()
    S = {}
    P = {}

    def sal(name, shape, dtype, side="left"):
        g = nc.sbuf_tensor(name, shape, dtype, side=side)
        h = g.__enter__()
        S[name] = (g, h)
        return h

    def sfree(*names):
        for name in names:
            g, _ = S.pop(name)
            g.__exit__(None, None, None)

    def pal(name, shape, dtype):
        g = nc.psum_tensor(name, shape, dtype)
        h = g.__enter__()
        P[name] = (g, h)
        return h

    def pfree(*names):
        for name in names:
            g, _ = P.pop(name)
            g.__exit__(None, None, None)

    def dma(eng, dst, src, tag=""):
        return em.op(eng, lambda e, dd=dst, ss=src: e.dma_start(dd, ss),
                     r=[src], w=[dst], kind='d', tag=tag)

    cop = em.op

    ident = sal("ident", [128, 128], BF16)
    statsg = sal("statsg", [128, NCORES * 16], F32)
    cols = sal("cols", [128, 32], F32)
    iota_s = sal("iota_s", [NSLOT, E], F32)
    expand_s = sal("expand_s", [E, 128], BF16)
    emap_s = sal("emap_s", [NSLOT, 1], I32)
    emap_f = sal("emap_f", [NSLOT, 1], F32)
    rb_s = sal("rb_s", [NSLOT, 1], F32)
    bb_s = sal("bb_s", [128, 32], F32)
    m8 = sal("m8", [NSLOT, E], BF16)
    goff_sb = sal("goff_sb", [NCORES, 1], I32)
    cols2 = sal("cols2", [128, 32], F32)
    qh = [sal(f"qh{h}", [128, T], BF16) for h in range(HPC)]
    kk = sal("kk", [128, T], BF16)
    vt = sal("vt", [128, NT128 * 129], BF16)

    dma("sync", ident[:, :], id_in[:, :], tag="ld_ident")
    dma("sync", iota_s[:, :], iota_c[:, :], tag="ld_iota")
    dma("sync", expand_s[:, :], expand_c[:, :], tag="ld_expand")
    dma("sync", emap_s[:, :], emap_i[:, :], tag="ld_emap")
    dma("sync", rb_s[:, :], rb[:, :], tag="ld_rb")
    dma("sync", bb_s[:, :], bb_t[:, :], tag="ld_bb")
    dma("sync", goff_sb[:, :], goff_d[:, :], tag="ld_goff")
    cop("vector", lambda v: v.tensor_copy(emap_f[:, :], emap_s[:, :]),
        r=[emap_s[:, :]], w=[emap_f[:, :]], tag="emap_cast")
    cop("vector", lambda v: v.tensor_scalar(m8[:, :], iota_s[:, :], emap_f[:, 0:1],
                                            None, ALU.is_equal),
        r=[iota_s[:, :], emap_f[:, 0:1]], w=[m8[:, :]], tag="onehot")
    epsc = cols[:, 30:31]
    cop("vector", lambda v: v.memset(epsc, EPS), w=[epsc], tag="eps")
    vt_ones = vt[:, :].rearrange("p (n o) -> p n o", o=129)[:, :, 128:129]
    cop("vector", lambda v, a=vt_ones: v.memset(a, 1.0), w=[vt[:, :]], tag="vt_ones")

    wqs = sal("wqs", [128, 32 * HPC * HD], BF16, side="right")
    wks = sal("wks", [128, 32 * HD], BF16, side="right")
    wvs = sal("wvs", [128, 32 * HD], BF16, side="right")
    ln1t = [sal("ln1t0", [128, 32 * 256], BF16, side="right")]
    xb2 = [sal(f"xb{i}", [128, 2 * D], BF16, side="right") for i in range(2)]
    xb = xb2[0]
    vf = sal("vf", [128, 256], BF16, side="right")
    stats = sal("stats", [128, 16], F32, side="right")

    dma("gpsimd", wqs[:, :].rearrange("p (dc o) -> p dc o", dc=32),
        wq_my[:, :].rearrange("(dc p) o -> p dc o", p=128), tag="ld_wq")
    dma("gpsimd", wks[:, :].rearrange("p (dc o) -> p dc o", dc=32),
        wk_my[:, :].rearrange("(dc p) o -> p dc o", p=128), tag="ld_wk")
    dma("gpsimd", wvs[:, :].rearrange("p (dc o) -> p dc o", dc=32),
        wv_my[:, :].rearrange("(dc p) o -> p dc o", p=128), tag="ld_wv")

    xt = xb2[1][:, :].bitcast(F32)
    sq1 = ln1t[0][:, :].bitcast(F32)
    cop("vector", lambda v: v.memset(stats[:, :], 0.0), w=[stats[:, :]], tag="stats0")
    for tt in range(OT):
        dma("sync", xt, x_own[tt * 128:(tt + 1) * 128, :], tag=f"p1x{tt}")
        c0, c1 = cols[:, 0:1], cols[:, 1:2]
        cop("vector", lambda v, c0=c0: v.reduce_sum(c0, xt, axis=AX.X),
            r=[xt], w=[c0], tag=f"p1s{tt}")
        cop("scalar", lambda s, c1=c1: s.activation(sq1, xt, AF.Square, accum_out=c1),
            r=[xt], w=[sq1, c1], tag=f"p1q{tt}")
        mcol = stats[:, tt * 2:tt * 2 + 1]
        rcol = stats[:, tt * 2 + 1:tt * 2 + 2]
        c2, c3, c4, c5 = cols[:, 2:3], cols[:, 3:4], cols[:, 4:5], cols[:, 5:6]
        cop("vector", lambda v, c0=c0, mcol=mcol: v.tensor_scalar(
            mcol, c0, 1.0 / D, None, ALU.mult), r=[c0], w=[mcol], tag=f"p1m{tt}")
        cop("vector", lambda v, c1=c1, c2=c2: v.tensor_scalar(
            c2, c1, 1.0 / D, None, ALU.mult), r=[c1], w=[c2], tag=f"p1a{tt}")
        cop("vector", lambda v, mcol=mcol, c3=c3: v.tensor_tensor(
            c3, mcol, mcol, ALU.mult), r=[mcol], w=[c3], tag=f"p1b{tt}")
        cop("vector", lambda v, c2=c2, c3=c3, c4=c4: v.tensor_tensor(
            c4, c2, c3, ALU.subtract), r=[c2, c3], w=[c4], tag=f"p1c{tt}")
        cop("scalar", lambda s, c4=c4, c5=c5: s.activation(c5, c4, AF.Sqrt, bias=epsc),
            r=[c4, epsc], w=[c5], tag=f"p1d{tt}")
        cop("vector", lambda v, c5=c5, rcol=rcol: v.reciprocal(rcol, c5),
            r=[c5], w=[rcol], tag=f"p1e{tt}")
    dma("sync", stats_ctr[:, :], stats[:, :], tag="stats_out")
    cop("gpsimd", lambda g: g.collective_compute(
        "AllGather", ALU.bypass, replica_groups=RG,
        ins=[stats_ctr.ap().opt()], outs=[stats_g.ap().opt()]),
        r=[stats_ctr[:, :]], w=[stats_g[:, :]], kind='cc', tag="ag_stats")
    dma("sync", statsg[:, :].rearrange("p (r s) -> p r s", r=NCORES),
        stats_g[:, :].rearrange("(r p) s -> p r s", p=128), tag="statsg_in")

    ptr = [pal(f"ptr{i}", [128, 1024], BF16) for i in range(2)]
    pqkv = [pal(f"pqkv{i}", [128, 256], F32) for i in range(6)]

    def statcols(g128):
        rr, tt = g128 // OT, g128 % OT
        base = rr * 16 + tt * 2
        return statsg[:, base:base + 1], statsg[:, base + 1:base + 2]

    def p2_load(g):
        xv = xb2[g % 2]
        dma("gpsimd", xv[:, :].rearrange("p (u d) -> p u d", u=2),
            x_full[g * 256:(g + 1) * 256, :].rearrange("(u p) d -> p u d", p=128),
            tag=f"p2x{g}")
        for u in range(2):
            mcol, rcol = statcols(g * 2 + u)
            seg = xv[:, u * D:(u + 1) * D]
            cop("vector", lambda v, seg=seg, mcol=mcol, rcol=rcol: v.tensor_scalar(
                seg, seg, mcol, rcol, ALU.subtract, ALU.mult),
                r=[seg, mcol, rcol], w=[seg], tag=f"p2a{g}_{u}")

    def p2_tr(g):
        xv = xb2[g % 2]
        lt = ln1t[0]
        for dcq in range(8):
            pb = ptr[dcq % 2]
            for j in range(4):
                dc = dcq * 4 + j
                for u in range(2):
                    src = xv[:, u * D + dc * 128: u * D + dc * 128 + 128]
                    dst = pb[:, (j * 2 + u) * 128:(j * 2 + u) * 128 + 128]
                    cop("tensor", lambda t, src=src, dst=dst: t.transpose(
                        dst, src, ident[:, :]),
                        r=[src, ident[:, :]], w=[dst], tag=f"p2t{g}_{dc}_{u}")
            cop("vector", lambda v, pb=pb, lt=lt, dcq=dcq: v.tensor_copy(
                lt[:, dcq * 1024:(dcq + 1) * 1024], pb[:, :]),
                r=[pb[:, :]], w=[lt[:, dcq * 1024:(dcq + 1) * 1024]],
                tag=f"p2e{g}_{dcq}")

    def p2_mm(g):
        lt = ln1t[0]
        for oc in range(6):
            pq = pqkv[oc]
            for dc in range(32):
                if oc < 4:
                    lhsT = wqs[:, dc * 512 + oc * 128: dc * 512 + oc * 128 + 128]
                elif oc == 4:
                    lhsT = wks[:, dc * 128:(dc + 1) * 128]
                else:
                    lhsT = wvs[:, dc * 128:(dc + 1) * 128]
                rhs = lt[:, dc * 256:(dc + 1) * 256]
                cop("tensor", lambda t, pq=pq, lhsT=lhsT, rhs=rhs, dc=dc: t.matmul(
                    pq[:, :], lhsT, rhs, start=(dc == 0), stop=(dc == 31)),
                    r=[lhsT, rhs, pq[:, :]], w=[pq[:, :]], tag=f"p2m{g}_{oc}_{dc}")
            if oc < 4:
                dst = qh[oc][:, g * 256:(g + 1) * 256]
                cop("scalar", lambda s, pq=pq, dst=dst: s.copy(dst, pq[:, :]),
                    r=[pq[:, :]], w=[dst], tag=f"p2q{g}_{oc}")
            elif oc == 4:
                dst = kk[:, g * 256:(g + 1) * 256]
                cop("scalar", lambda s, pq=pq, dst=dst: s.copy(dst, pq[:, :]),
                    r=[pq[:, :]], w=[dst], tag=f"p2k{g}")
            else:
                cop("scalar", lambda s, pq=pq: s.copy(vf[:, :], pq[:, :]),
                    r=[pq[:, :]], w=[vf[:, :]], tag=f"p2v{g}")
                pb = ptr[g % 2]
                for u in range(2):
                    src = vf[:, u * 128:(u + 1) * 128]
                    dst = pb[:, u * 128:(u + 1) * 128]
                    cop("tensor", lambda t, src=src, dst=dst: t.transpose(
                        dst, src, ident[:, :]),
                        r=[src, ident[:, :]], w=[dst], tag=f"p2vt{g}_{u}")
                vdst = vt[:, :].rearrange("p (n o) -> p n o", o=129)[:, 2 * g:2 * g + 2, 0:128]
                cop("scalar", lambda s, pb=pb, vdst=vdst: s.copy(
                    vdst, pb[:, 0:256].rearrange("p (n o) -> p n o", o=128)),
                    r=[pb[:, 0:256]], w=[vt[:, 2 * g * 129:(2 * g + 2) * 129]],
                    tag=f"p2ve{g}")

    for g in range(NG2):
        p2_load(g)
        if g > 0:
            p2_mm(g - 1)
        p2_tr(g)
    p2_mm(NG2 - 1)

    em.barrier()
    sfree("stats", "vf", "xb1", "xb0", "ln1t0", "wvs", "wks", "wqs")
    pfree("pqkv5", "pqkv4", "pqkv3", "pqkv2", "pqkv1", "pqkv0", "ptr1", "ptr0")

    tri_s = sal("tri_s", [128, 4 * 512], BF16, side="right")
    attnm = sal("attnm", [128, NT128 * HPC * HD], BF16, side="right")
    pt = [sal(f"pt{i}", [128, 1024], BF16, side="right") for i in range(3)]
    dma("sync", tri_s[:, :], tri_c[:, :], tag="ld_tri")

    psc = [pal(f"psc{i}", [128, 1024], F32) for i in range(2)]
    pav = [pal(f"pav{i}", [128, 129], F32) for i in range(4)]

    def qk_exp(h, qc, kch, nkch, rot):
        q = qh[h]
        kc = kch // 2
        diag = (kc == qc)
        ps = psc[rot % 2]
        ptile = pt[rot % 3]
        for j2 in range(2):
            j = (kch % 2) * 2 + j2 if False else j2
            kb = kch * 256 + j2 * 128
            cop("tensor", lambda t, ps=ps, j2=j2, kb=kb, q=q, qc=qc: t.matmul(
                ps[:, j2 * 512:(j2 + 1) * 512], kk[:, kb:kb + 128],
                q[:, qc * 512:(qc + 1) * 512], start=True, stop=True),
                r=[kk[:, kb:kb + 128], q[:, qc * 512:(qc + 1) * 512],
                   ps[:, j2 * 512:(j2 + 1) * 512]],
                w=[ps[:, j2 * 512:(j2 + 1) * 512]], tag=f"qk{h}_{qc}_{kch}_{j2}")
        cop("scalar", lambda s, ptile=ptile, ps=ps: s.activation(
            ptile[:, :], ps[:, :], AF.Exp, scale=1.0 / math.sqrt(HD)),
            r=[ps[:, :]], w=[ptile[:, :]], tag=f"ex{h}_{qc}_{kch}")
        if diag:
            for j2 in range(2):
                j = (kch % 2) * 2 + j2
                pv = ptile[:, j2 * 512:(j2 + 1) * 512]
                tv = tri_s[:, j * 512:(j + 1) * 512]
                cop("vector", lambda v, pv=pv, tv=tv: v.tensor_tensor(
                    pv, pv, tv, ALU.mult),
                    r=[pv, tv], w=[pv], tag=f"mk{h}_{qc}_{kch}_{j2}")

    def av(h, qc, kch, nkch, rot):
        ptile = pt[rot % 3]
        for j2 in range(2):
            gt = kch * 2 + j2
            rhs = vt[:, gt * 129:(gt + 1) * 129]
            for qs in range(4):
                pa = pav[qs]
                lhsT = ptile[:, j2 * 512 + qs * 128: j2 * 512 + qs * 128 + 128]
                st = (kch == 0 and j2 == 0)
                sp = (kch == nkch - 1 and j2 == 1)
                cop("tensor", lambda t, pa=pa, lhsT=lhsT, rhs=rhs,
                    st=st, sp=sp: t.matmul(
                    pa[:, 0:129], lhsT, rhs, start=st, stop=sp),
                    r=[lhsT, rhs, pa[:, 0:129]],
                    w=[pa[:, 0:129]], tag=f"av{h}_{qc}_{kch}_{j2}_{qs}")

    CHK = [(0, NQC // 2), (NQC // 2, 3 * NQC // 4),
           (3 * NQC // 4, 7 * NQC // 8), (7 * NQC // 8, NQC)]

    def ag_chunk(k):
        q0, q1 = CHK[k]
        r0, r1 = q0 * 512, q1 * 512
        nrow = r1 - r0
        ctr_ap = attnm_ctr[r0:r1, :].rearrange("(n p) f -> p n f", p=128)
        src_ap = attnm[:, r0 // 128 * (HPC * HD): r1 // 128 * (HPC * HD)].rearrange(
            "p (n f) -> p n f", f=HPC * HD)
        dma("sync", ctr_ap, src_ap, tag=f"attnm_out{k}")
        g0 = sum((CHK[j][1] - CHK[j][0]) * 512 * NCORES for j in range(k))
        cop("gpsimd", lambda g: g.collective_compute(
            "AllGather", ALU.bypass, replica_groups=RG,
            ins=[attnm_ctr[r0:r1, :].opt()],
            outs=[attnm_g[g0:g0 + nrow * NCORES, :].opt()]),
            r=[attnm_ctr[r0:r1, :]], w=[attnm_g[g0:g0 + nrow * NCORES, :]],
            kind='cc', tag=f"ag_attnm{k}")

    rot = [0]
    for qc in range(NQC):
        nkch = 2 * (qc + 1)
        for h in range(HPC):
            base = rot[0]
            for kch in range(nkch):
                qk_exp(h, qc, kch, nkch, base + kch)
                if kch >= 2:
                    av(h, qc, kch - 2, nkch, base + kch - 2)
            av(h, qc, nkch - 2, nkch, base + nkch - 2)
            av(h, qc, nkch - 1, nkch, base + nkch - 1)
            rot[0] += nkch
            for qs in range(4):
                pa = pav[qs]
                rc = cols[:, 8 + qs:9 + qs]
                cop("vector", lambda v, pa=pa, rc=rc: v.reciprocal(
                    rc, pa[:, 128:129]),
                    r=[pa[:, 128:129]], w=[rc], tag=f"lr{h}_{qc}_{qs}")
                gt = qc * 4 + qs
                dst = attnm[:, gt * (HPC * HD) + h * HD: gt * (HPC * HD) + h * HD + HD]
                cop("vector", lambda v, dst=dst, pa=pa, rc=rc: v.tensor_scalar(
                    dst, pa[:, 0:128], rc, None, ALU.mult),
                    r=[pa[:, 0:128], rc], w=[dst], tag=f"an{h}_{qc}_{qs}")
        for k in range(len(CHK)):
            if qc == CHK[k][1] - 1:
                ag_chunk(k)

    em.barrier()
    sfree("pt2", "pt1", "pt0", "attnm", "tri_s")
    pfree("pav3", "pav2", "pav1", "pav0", "psc1", "psc0")
    sfree("vt", "kk", "qh3", "qh2", "qh1", "qh0")

    att = sal("att", [128, 32 * TPC], BF16)
    atg = sal("atg", [128, OT * D], BF16, side="right")
    FW = HPC * HD

    _offs = {}

    def mk_setup(e):
        for rr in range(NCORES):
            _offs[rr] = e.bass.values_load(
                goff_sb[rr:rr + 1, 0:1], engines=[e.engine], min_val=0,
                max_val=T * NCORES - TPC, skip_runtime_bounds_check=True)
        return e.nop()
    em.op("sync", mk_setup, r=[goff_sb[:, :]], tag="atg_setup")
    for rr in range(NCORES):
        def mk(e, rr=rr):
            src = attnm_g[bass.ds(_offs[rr], TPC), :].rearrange("(n p) f -> p n f", p=128)
            dstv = atg[:, :].rearrange("p (n d) -> p n d", d=D)[
                :, :, rr * FW:(rr + 1) * FW]
            return e.dma_start(dstv, src)
        em.op("sync", mk, r=[attnm_g[:, :]], w=[atg[:, :]], kind='d', tag=f"atg{rr}")

    ptr2 = [pal(f"ptr2_{i}", [128, 1024], BF16) for i in range(2)]
    for fc in range(32):
        pb = ptr2[fc % 2]
        for tt in range(OT):
            src = atg[:, tt * D + fc * 128: tt * D + fc * 128 + 128]
            dst = pb[:, tt * 128:(tt + 1) * 128]
            cop("tensor", lambda t, src=src, dst=dst: t.transpose(dst, src, ident[:, :]),
                r=[src, ident[:, :]], w=[dst], tag=f"p4t{fc}_{tt}")
        cop("vector", lambda v, pb=pb, fc=fc: v.tensor_copy(
            att[:, fc * TPC:(fc + 1) * TPC], pb[:, 0:TPC]),
            r=[pb[:, 0:TPC]], w=[att[:, fc * TPC:(fc + 1) * TPC]], tag=f"p4e{fc}")

    em.barrier()
    sfree("atg")
    pfree("ptr2_1", "ptr2_0")

    y1m = sal("y1m", [128, OT * D], BF16, side="right")
    wos = [sal(f"wos{i}", [128, 32 * 128], BF16, side="right") for i in range(2)]
    ystg = [sal(f"ystg{i}", [128, 512], BF16, side="right") for i in range(2)]

    py = [pal(f"py{i}", [128, 512], F32) for i in range(2)]
    ptr3 = [pal(f"ptr3_{i}", [128, 1024], BF16) for i in range(2)]

    def proj_pass(w_src, rhs_buf, dst_m, bias_cols, lora, tagp):
        backlog = []

        def tail(oc, th, pq, i):
            ys = ystg[i % 2]
            if bias_cols is not None:
                bc = bias_cols[:, oc:oc + 1]
                cop("scalar", lambda s, pq=pq, bc=bc, ys=ys: s.add(ys[:, :], pq[:, :], bc),
                    r=[pq[:, :], bc], w=[ys[:, :]], tag=f"{tagp}v{oc}_{th}")
            else:
                cop("scalar", lambda s, pq=pq, ys=ys: s.copy(ys[:, :], pq[:, :]),
                    r=[pq[:, :]], w=[ys[:, :]], tag=f"{tagp}v{oc}_{th}")
            pb = ptr3[i % 2]
            for u in range(4):
                src = ys[:, u * 128:(u + 1) * 128]
                dst = pb[:, u * 128:(u + 1) * 128]
                cop("tensor", lambda t, src=src, dst=dst: t.transpose(dst, src, ident[:, :]),
                    r=[src, ident[:, :]], w=[dst], tag=f"{tagp}t{oc}_{th}_{u}")
            dstv = dst_m[:, :].rearrange("p (n d) -> p n d", d=D)[
                :, th * 4:(th + 1) * 4, oc * 128:(oc + 1) * 128]
            cop("vector", lambda v, pb=pb, dstv=dstv: v.tensor_copy(
                dstv, pb[:, 0:512].rearrange("p (n o) -> p n o", o=128)),
                r=[pb[:, 0:512]], w=[dst_m[:, th * 4 * D:(th + 1) * 4 * D]],
                tag=f"{tagp}w{oc}_{th}")

        i = 0
        for oc in range(32):
            wb = wos[oc % 2]
            dma("gpsimd", wb[:, :],
                w_src[oc].rearrange("p dc o -> p (dc o)"), tag=f"{tagp}l{oc}")
            for th in range(NTH):
                pq = py[i % 2]
                for dc in range(32):
                    lhsT = wb[:, dc * 128:(dc + 1) * 128]
                    rhs = rhs_buf[:, dc * TPC + th * 512: dc * TPC + th * 512 + 512]
                    cop("tensor", lambda t, pq=pq, lhsT=lhsT, rhs=rhs, dc=dc: t.matmul(
                        pq[:, :], lhsT, rhs, start=(dc == 0), stop=(dc == 31 and lora is None)),
                        r=[lhsT, rhs, pq[:, :]], w=[pq[:, :]], tag=f"{tagp}m{oc}_{th}_{dc}")
                if lora is not None:
                    bt_s, hw_s = lora
                    lhsT = bt_s[:, oc * 128:(oc + 1) * 128]
                    rhs = hw_s[:, th * 512:(th + 1) * 512]
                    cop("tensor", lambda t, pq=pq, lhsT=lhsT, rhs=rhs: t.matmul(
                        pq[:, :], lhsT, rhs, start=False, stop=True),
                        r=[lhsT, rhs, pq[:, :]], w=[pq[:, :]], tag=f"{tagp}lo{oc}_{th}")
                backlog.append((oc, th, pq, i))
                if len(backlog) > 1:
                    tail(*backlog.pop(0))
                i += 1
        while backlog:
            tail(*backlog.pop(0))

    proj_pass(wo, att, y1m, None, None, "op")

    em.barrier()
    sfree("ystg1", "ystg0", "wos1", "wos0")
    sfree("att")
    ln2t = sal("ln2t", [128, 32 * TPC], BF16)
    xt2 = [sal(f"xt2_{i}", [128, D], F32, side="right") for i in range(2)]
    t1f = sal("t1f", [128, D], F32, side="right")
    junk2a = [sal(f"junk2_{i}", [128, D], BF16, side="right") for i in range(2)]

    def ln_residual(ym, src_x, dst_x, ln_dst, tagp):
        def stages(tt):
            CS = cols if tt % 2 == 0 else cols2
            xv = xt2[tt % 2]
            junk2 = junk2a[tt % 2]
            yv = ym[:, tt * D:(tt + 1) * D]
            c0, c1 = CS[:, 0:1], CS[:, 1:2]
            cm, ca, cb, cv, cs_, cr, cmr = (CS[:, 16:17], CS[:, 17:18], CS[:, 18:19],
                                            CS[:, 19:20], CS[:, 20:21], CS[:, 21:22],
                                            CS[:, 22:23])
            c6, c7 = CS[:, 6:7], CS[:, 7:8]
            d0, d1, d2, d3, d4, d5 = (CS[:, 23:24], CS[:, 24:25], CS[:, 25:26],
                                      CS[:, 26:27], CS[:, 27:28], CS[:, 28:29])

            def s0():
                dma("sync", xv[:, :], src_x[tt * 128:(tt + 1) * 128, :], tag=f"{tagp}x{tt}")
                cop("vector", lambda v: v.reduce_sum(c0, yv, axis=AX.X),
                    r=[yv], w=[c0], tag=f"{tagp}s{tt}")
                cop("scalar", lambda s: s.activation(t1f[:, :], yv, AF.Square, accum_out=c1),
                    r=[yv], w=[t1f[:, :], c1], tag=f"{tagp}q{tt}")

            def s1():
                cop("vector", lambda v: v.tensor_scalar(cm, c0, 1.0 / D, None, ALU.mult),
                    r=[c0], w=[cm], tag=f"{tagp}m{tt}")
                cop("vector", lambda v: v.tensor_scalar(ca, c1, 1.0 / D, None, ALU.mult),
                    r=[c1], w=[ca], tag=f"{tagp}a{tt}")
                cop("vector", lambda v: v.tensor_tensor(cb, cm, cm, ALU.mult),
                    r=[cm], w=[cb], tag=f"{tagp}b{tt}")
                cop("vector", lambda v: v.tensor_tensor(cv, ca, cb, ALU.subtract),
                    r=[ca, cb], w=[cv], tag=f"{tagp}c{tt}")
                cop("scalar", lambda s: s.activation(cs_, cv, AF.Sqrt, bias=epsc),
                    r=[cv, epsc], w=[cs_], tag=f"{tagp}d{tt}")

            def s2():
                cop("vector", lambda v: v.reciprocal(cr, cs_),
                    r=[cs_], w=[cr], tag=f"{tagp}e{tt}")
                cop("vector", lambda v: v.tensor_tensor(cmr, cm, cr, ALU.mult),
                    r=[cm, cr], w=[cmr], tag=f"{tagp}f{tt}")

            def s3():
                cop("vector", lambda v: v.tensor_scalar(
                    t1f[:, :], xv[:, :], cmr, None, ALU.subtract),
                    r=[xv[:, :], cmr], w=[t1f[:, :]], tag=f"{tagp}g{tt}")
                cop("vector", lambda v: v.scalar_tensor_tensor(
                    xv[:, :], yv, cr, t1f[:, :], ALU.mult, ALU.add),
                    r=[yv, cr, t1f[:, :]], w=[xv[:, :]], tag=f"{tagp}h{tt}")
                dma("sync", dst_x[tt * 128:(tt + 1) * 128, :], xv[:, :], tag=f"{tagp}o{tt}")
                cop("vector", lambda v: v.reduce_sum(c6, xv[:, :], axis=AX.X),
                    r=[xv[:, :]], w=[c6], tag=f"{tagp}2s{tt}")
                cop("scalar", lambda s: s.activation(t1f[:, :], xv[:, :], AF.Square, accum_out=c7),
                    r=[xv[:, :]], w=[t1f[:, :], c7], tag=f"{tagp}2q{tt}")

            def s4():
                cop("vector", lambda v: v.tensor_scalar(d0, c6, 1.0 / D, None, ALU.mult),
                    r=[c6], w=[d0], tag=f"{tagp}2m{tt}")
                cop("vector", lambda v: v.tensor_scalar(d1, c7, 1.0 / D, None, ALU.mult),
                    r=[c7], w=[d1], tag=f"{tagp}2a{tt}")
                cop("vector", lambda v: v.tensor_tensor(d2, d0, d0, ALU.mult),
                    r=[d0], w=[d2], tag=f"{tagp}2b{tt}")
                cop("vector", lambda v: v.tensor_tensor(d3, d1, d2, ALU.subtract),
                    r=[d1, d2], w=[d3], tag=f"{tagp}2c{tt}")
                cop("scalar", lambda s: s.activation(d4, d3, AF.Sqrt, bias=epsc),
                    r=[d3, epsc], w=[d4], tag=f"{tagp}2d{tt}")

            def s5():
                cop("vector", lambda v: v.reciprocal(d5, d4),
                    r=[d4], w=[d5], tag=f"{tagp}2e{tt}")
                cop("vector", lambda v: v.tensor_scalar(
                    junk2[:, :], xv[:, :], d0, d5, ALU.subtract, ALU.mult),
                    r=[xv[:, :], d0, d5], w=[junk2[:, :]], tag=f"{tagp}2f{tt}")

            def s6():
                for bq in range(4):
                    pb = ptr3[bq % 2]
                    for u in range(8):
                        dc = bq * 8 + u
                        src2 = junk2[:, dc * 128:(dc + 1) * 128]
                        dst2 = pb[:, u * 128:(u + 1) * 128]
                        cop("tensor", lambda t, src2=src2, dst2=dst2: t.transpose(
                            dst2, src2, ident[:, :]),
                            r=[src2, ident[:, :]], w=[dst2], tag=f"{tagp}2t{tt}_{dc}")
                    dstv = ln_dst[:, :].rearrange("p (c t) -> p c t", t=TPC)[
                        :, bq * 8:(bq + 1) * 8, tt * 128:(tt + 1) * 128]
                    cop("vector", lambda v, pb=pb, dstv=dstv: v.tensor_copy(
                        dstv, pb[:, :].rearrange("p (n o) -> p n o", o=128)),
                        r=[pb[:, :]], w=[ln_dst[:, :]], tag=f"{tagp}2w{tt}_{bq}")
            return [s0, s1, s2, s3, s4, s5, s6]

        for base in range(0, OT, 2):
            a = stages(base)
            b = stages(base + 1) if base + 1 < OT else None
            for si in range(7):
                a[si]()
                if b is not None:
                    b[si]()

    ln_residual(y1m, x_own, x2_d, ln2t, "p4")

    em.barrier()
    sfree("junk2_1", "junk2_0", "t1f", "xt2_1", "xt2_0", "y1m")
    pfree("ptr3_1", "ptr3_0", "py1", "py0")

    y2m = sal("y2m", [128, OT * D], BF16, side="right")
    bts = sal("bts", [128, D], BF16, side="right")
    hw = sal("hw", [128, TPC], BF16, side="right")
    rt = sal("rt", [128, 32 * 16], BF16, side="right")
    at = sal("at", [128, 32 * 128], BF16, side="right")
    identf = sal("identf", [128, 128], F32, side="right")
    lg = sal("lg", [NSLOT, TPC], F32, side="right")
    lgt = sal("lgt", [128, OT * 16], F32, side="right")
    probs = sal("probs", [128, 16], F32, side="right")
    gcnt = sal("gcnt", [128, 16], F32, side="right")
    sel = sal("sel", [128, 16], F32, side="right")
    sps = sal("sps", [NSLOT, TPC], BF16, side="right")
    w8 = sal("w8", [E, TPC], BF16, side="right")
    wer = sal("wer", [128, TPC], BF16, side="right")

    dma("gpsimd", rt[:, :].rearrange("p (dc o) -> p dc o", dc=32),
        rw_t[:, :].rearrange("(dc p) o -> p dc o", p=128), tag="ld_rt")
    dma("gpsimd", at[:, :].rearrange("p (dc o) -> p dc o", dc=32),
        a_t[:, :].rearrange("(dc p) o -> p dc o", p=128), tag="ld_at")
    dma("gpsimd", bts[:, :], b_t[:, :], tag="ld_bt")
    cop("vector", lambda v: v.tensor_copy(identf[:, :], ident[:, :]),
        r=[ident[:, :]], w=[identf[:, :]], tag="identf")

    plg = [pal(f"plg{i}", [NSLOT, 512], F32) for i in range(2)]
    ptr4 = [pal(f"ptr4_{i}", [128, 512], F32) for i in range(2)]

    for th in range(NTH):
        pq = plg[th % 2]
        for dc in range(32):
            lhsT = rt[:, dc * 16:dc * 16 + NSLOT]
            rhs = ln2t[:, dc * TPC + th * 512: dc * TPC + th * 512 + 512]
            cop("tensor", lambda t, pq=pq, lhsT=lhsT, rhs=rhs, dc=dc: t.matmul(
                pq[:, :], lhsT, rhs, start=(dc == 0), stop=(dc == 31)),
                r=[lhsT, rhs, pq[:, :]], w=[pq[:, :]], tag=f"p5r{th}_{dc}")
        dst = lg[:, th * 512:(th + 1) * 512]
        cop("scalar", lambda s, pq=pq, dst=dst: s.add(dst, pq[:, :], rb_s[:, 0:1]),
            r=[pq[:, :], rb_s[:, 0:1]], w=[dst], tag=f"p5re{th}")
    for tt in range(OT):
        pb = ptr4[tt % 2]
        src = lg[:, tt * 128:(tt + 1) * 128]
        cop("tensor", lambda t, pb=pb, src=src: t.transpose(
            pb[:, 0:NSLOT], src, identf[0:NSLOT, 0:NSLOT]),
            r=[src, identf[:, :]], w=[pb[:, 0:16]], tag=f"p5lt{tt}")
        dst = lgt[:, tt * 16:tt * 16 + NSLOT]
        cop("scalar", lambda s, pb=pb, dst=dst: s.copy(dst, pb[:, 0:NSLOT]),
            r=[pb[:, 0:16]], w=[dst], tag=f"p5le{tt}")
    for tt in range(OT):
        lv = lgt[:, tt * 16:tt * 16 + NSLOT]
        c0, c1, c2, c3 = cols[:, 0:1], cols[:, 1:2], cols[:, 2:3], cols[:, 3:4]
        cop("vector", lambda v, lv=lv, c0=c0: v.reduce_max(c0, lv, axis=AX.X),
            r=[lv], w=[c0], tag=f"p5mx{tt}")
        cop("vector", lambda v, c0=c0, c1=c1: v.tensor_scalar(
            c1, c0, -1.0, None, ALU.mult), r=[c0], w=[c1], tag=f"p5ng{tt}")
        cop("scalar", lambda s, lv=lv, c1=c1: s.activation(
            probs[:, 0:NSLOT], lv, AF.Exp, bias=c1),
            r=[lv, c1], w=[probs[:, 0:NSLOT]], tag=f"p5ex{tt}")
        cop("vector", lambda v, c2=c2: v.reduce_sum(c2, probs[:, 0:NSLOT], axis=AX.X),
            r=[probs[:, 0:NSLOT]], w=[c2], tag=f"p5sm{tt}")
        cop("vector", lambda v, c2=c2, c3=c3: v.reciprocal(c3, c2),
            r=[c2], w=[c3], tag=f"p5rc{tt}")
        cop("vector", lambda v, c3=c3: v.tensor_scalar(
            probs[:, 0:NSLOT], probs[:, 0:NSLOT], c3, None, ALU.mult),
            r=[probs[:, 0:NSLOT], c3], w=[probs[:, 0:NSLOT]], tag=f"p5pn{tt}")
        cop("vector", lambda v: v.memset(gcnt[:, 0:NSLOT], 0.0),
            w=[gcnt[:, 0:NSLOT]], tag=f"p5g0{tt}")
        for sl in range(NSLOT):
            pc = probs[:, sl:sl + 1]
            cop("vector", lambda v, pc=pc: v.tensor_scalar(
                sel[:, 0:NSLOT], probs[:, 0:NSLOT], pc, None, ALU.is_lt),
                r=[probs[:, 0:NSLOT], pc], w=[sel[:, 0:NSLOT]], tag=f"p5c{tt}_{sl}")
            cop("vector", lambda v: v.tensor_tensor(
                gcnt[:, 0:NSLOT], gcnt[:, 0:NSLOT], sel[:, 0:NSLOT], ALU.add),
                r=[gcnt[:, 0:NSLOT], sel[:, 0:NSLOT]], w=[gcnt[:, 0:NSLOT]],
                tag=f"p5a{tt}_{sl}")
        cop("vector", lambda v: v.tensor_scalar(
            sel[:, 0:NSLOT], gcnt[:, 0:NSLOT], float(E) - 0.5, None, ALU.is_lt),
            r=[gcnt[:, 0:NSLOT]], w=[sel[:, 0:NSLOT]], tag=f"p5sl{tt}")
        cop("vector", lambda v: v.tensor_tensor(
            probs[:, 0:NSLOT], probs[:, 0:NSLOT], sel[:, 0:NSLOT], ALU.mult),
            r=[probs[:, 0:NSLOT], sel[:, 0:NSLOT]], w=[probs[:, 0:NSLOT]], tag=f"p5sp{tt}"),
        pb = ptr4[tt % 2]
        cop("tensor", lambda t, pb=pb: t.transpose(
            pb[0:NSLOT, 0:128], probs[:, 0:NSLOT], identf[:, :]),
            r=[probs[:, 0:NSLOT], identf[:, :]], w=[pb[:, 0:128]], tag=f"p5st{tt}")
        dst = sps[:, tt * 128:(tt + 1) * 128]
        cop("scalar", lambda s, pb=pb, dst=dst: s.copy(dst, pb[0:NSLOT, 0:128]),
            r=[pb[:, 0:128]], w=[dst], tag=f"p5se{tt}")

    pfree("ptr4_1", "ptr4_0", "plg1", "plg0")
    pw8 = pal("pw8", [E, 512], F32)
    pwer = pal("pwer", [128, 512], F32)
    ph = [pal(f"ph{i}", [128, 512], F32) for i in range(2)]

    for th in range(NTH):
        rhs = sps[:, th * 512:(th + 1) * 512]
        cop("tensor", lambda t, rhs=rhs: t.matmul(pw8[:, :], m8[:, :], rhs,
                                                  start=True, stop=True),
            r=[m8[:, :], rhs, pw8[:, :]], w=[pw8[:, :]], tag=f"p5w8{th}")
        dst = w8[:, th * 512:(th + 1) * 512]
        cop("scalar", lambda s, dst=dst: s.copy(dst, pw8[:, :]),
            r=[pw8[:, :]], w=[dst], tag=f"p5w8e{th}")
        rhs2 = w8[:, th * 512:(th + 1) * 512]
        cop("tensor", lambda t, rhs2=rhs2: t.matmul(pwer[:, :], expand_s[:, :], rhs2,
                                                    start=True, stop=True),
            r=[expand_s[:, :], rhs2, pwer[:, :]], w=[pwer[:, :]], tag=f"p5we{th}")
        dst2 = wer[:, th * 512:(th + 1) * 512]
        cop("scalar", lambda s, dst2=dst2: s.copy(dst2, pwer[:, :]),
            r=[pwer[:, :]], w=[dst2], tag=f"p5wee{th}")
        pq3 = ph[th % 2]
        for dc in range(32):
            lhsT = at[:, dc * 128:(dc + 1) * 128]
            rhs3 = ln2t[:, dc * TPC + th * 512: dc * TPC + th * 512 + 512]
            cop("tensor", lambda t, pq3=pq3, lhsT=lhsT, rhs3=rhs3, dc=dc: t.matmul(
                pq3[:, :], lhsT, rhs3, start=(dc == 0), stop=(dc == 31)),
                r=[lhsT, rhs3, pq3[:, :]], w=[pq3[:, :]], tag=f"p5h{th}_{dc}")
        dst3 = hw[:, th * 512:(th + 1) * 512]
        w_in = wer[:, th * 512:(th + 1) * 512]
        cop("vector", lambda v, pq3=pq3, dst3=dst3, w_in=w_in: v.tensor_tensor(
            dst3, pq3[:, :], w_in, ALU.mult),
            r=[pq3[:, :], w_in], w=[dst3], tag=f"p5hw{th}")

    em.barrier()
    pfree("ph1", "ph0", "pwer", "pw8")
    sfree("wer", "w8", "sps", "sel", "gcnt", "probs", "lgt", "lg", "identf",
          "at", "rt")
    py2 = [pal(f"py2_{i}", [128, 512], F32) for i in range(2)]
    ptr5 = [pal(f"ptr5_{i}", [128, 1024], BF16) for i in range(5)]
    wos2 = [sal(f"wos2_{i}", [128, 32 * 128], BF16, side="right") for i in range(2)]
    ystg2 = [sal(f"ystg2_{i}", [128, 512], BF16, side="right") for i in range(2)]
    xt3 = sal("xt3", [128, D], F32, side="right")
    t1g = sal("t1g", [128, D], F32, side="right")

    def proj_pass2(w_src, rhs_buf, dst_m, bias_cols, lora, tagp, th_list):
        backlog = []

        def tail(oc, th, pq, i):
            ys = ystg2[i % 2]
            bc = bias_cols[:, oc:oc + 1]
            cop("scalar", lambda s, pq=pq, bc=bc, ys=ys: s.add(ys[:, :], pq[:, :], bc),
                r=[pq[:, :], bc], w=[ys[:, :]], tag=f"{tagp}v{oc}_{th}")
            pb = ptr5[i % 5]
            for u in range(4):
                src = ys[:, u * 128:(u + 1) * 128]
                dst = pb[:, u * 128:(u + 1) * 128]
                cop("tensor", lambda t, src=src, dst=dst: t.transpose(dst, src, ident[:, :]),
                    r=[src, ident[:, :]], w=[dst], tag=f"{tagp}t{oc}_{th}_{u}")
            dstv = dst_m[:, :].rearrange("p (n d) -> p n d", d=D)[
                :, th * 4:(th + 1) * 4, oc * 128:(oc + 1) * 128]
            cop("vector", lambda v, pb=pb, dstv=dstv: v.tensor_copy(
                dstv, pb[:, 0:512].rearrange("p (n o) -> p n o", o=128)),
                r=[pb[:, 0:512]], w=[dst_m[:, th * 4 * D:(th + 1) * 4 * D]],
                tag=f"{tagp}w{oc}_{th}")

        i = 0
        for th in th_list:
            for oc in range(32):
                wb = wos2[i % 2]
                dma("gpsimd", wb[:, :],
                    w_src[oc].rearrange("p dc o -> p (dc o)"), tag=f"{tagp}l{th}_{oc}")
                pq = py2[i % 2]
                for dc in range(32):
                    lhsT = wb[:, dc * 128:(dc + 1) * 128]
                    rhs = rhs_buf[:, dc * TPC + th * 512: dc * TPC + th * 512 + 512]
                    cop("tensor", lambda t, pq=pq, lhsT=lhsT, rhs=rhs, dc=dc: t.matmul(
                        pq[:, :], lhsT, rhs, start=(dc == 0), stop=False),
                        r=[lhsT, rhs, pq[:, :]], w=[pq[:, :]], tag=f"{tagp}m{oc}_{th}_{dc}")
                bt_s, hw_s = lora
                lhsT = bt_s[:, oc * 128:(oc + 1) * 128]
                rhs = hw_s[:, th * 512:(th + 1) * 512]
                cop("tensor", lambda t, pq=pq, lhsT=lhsT, rhs=rhs: t.matmul(
                    pq[:, :], lhsT, rhs, start=False, stop=True),
                    r=[lhsT, rhs, pq[:, :]], w=[pq[:, :]], tag=f"{tagp}lo{oc}_{th}")
                backlog.append((oc, th, pq, i))
                if len(backlog) > 1:
                    tail(*backlog.pop(0))
                i += 1
        while backlog:
            tail(*backlog.pop(0))

    def final_res_th(ym, th_list, tagp):
        tts = [th * 4 + i for th in th_list for i in range(4)]
        for tt in tts:
            CS = cols if tt % 2 == 0 else cols2
            xv = xt3
            dma("sync", xv[:, :], x2_d[tt * 128:(tt + 1) * 128, :], tag=f"{tagp}x{tt}")
            yv = ym[:, tt * D:(tt + 1) * D]
            c0, c1 = CS[:, 0:1], CS[:, 1:2]
            cop("vector", lambda v, yv=yv, c0=c0: v.reduce_sum(c0, yv, axis=AX.X),
                r=[yv], w=[c0], tag=f"{tagp}s{tt}")
            cop("scalar", lambda s, yv=yv, c1=c1: s.activation(
                t1g[:, :], yv, AF.Square, accum_out=c1),
                r=[yv], w=[t1g[:, :], c1], tag=f"{tagp}q{tt}")
            cm, ca, cb, cv, cs_, cr, cmr = (CS[:, 16:17], CS[:, 17:18], CS[:, 18:19],
                                            CS[:, 19:20], CS[:, 20:21], CS[:, 21:22],
                                            CS[:, 22:23])
            cop("vector", lambda v, c0=c0, cm=cm: v.tensor_scalar(
                cm, c0, 1.0 / D, None, ALU.mult), r=[c0], w=[cm], tag=f"{tagp}m{tt}")
            cop("vector", lambda v, c1=c1, ca=ca: v.tensor_scalar(
                ca, c1, 1.0 / D, None, ALU.mult), r=[c1], w=[ca], tag=f"{tagp}a{tt}")
            cop("vector", lambda v, cm=cm, cb=cb: v.tensor_tensor(
                cb, cm, cm, ALU.mult), r=[cm], w=[cb], tag=f"{tagp}b{tt}")
            cop("vector", lambda v, ca=ca, cb=cb, cv=cv: v.tensor_tensor(
                cv, ca, cb, ALU.subtract), r=[ca, cb], w=[cv], tag=f"{tagp}c{tt}")
            cop("scalar", lambda s, cv=cv, cs_=cs_: s.activation(cs_, cv, AF.Sqrt, bias=epsc),
                r=[cv, epsc], w=[cs_], tag=f"{tagp}d{tt}")
            cop("vector", lambda v, cs_=cs_, cr=cr: v.reciprocal(cr, cs_),
                r=[cs_], w=[cr], tag=f"{tagp}e{tt}")
            cop("vector", lambda v, cm=cm, cr=cr, cmr=cmr: v.tensor_tensor(
                cmr, cm, cr, ALU.mult), r=[cm, cr], w=[cmr], tag=f"{tagp}f{tt}")
            cop("vector", lambda v, xv=xv, cmr=cmr: v.tensor_scalar(
                t1g[:, :], xv[:, :], cmr, None, ALU.subtract),
                r=[xv[:, :], cmr], w=[t1g[:, :]], tag=f"{tagp}g{tt}")
            cop("vector", lambda v, yv=yv, cr=cr, xv=xv: v.scalar_tensor_tensor(
                xv[:, :], yv, cr, t1g[:, :], ALU.mult, ALU.add),
                r=[yv, cr, t1g[:, :]], w=[xv[:, :]], tag=f"{tagp}h{tt}")
            dma("sync", out_d[tt * 128:(tt + 1) * 128, :], xv[:, :], tag=f"{tagp}o{tt}")

    proj_pass2(bw_t, ln2t, y2m, bb_s, (bts, hw), "p5a", [0])
    final_res_th(y2m, [0], "p6a")
    if NTH > 1:
        proj_pass2(bw_t, ln2t, y2m, bb_s, (bts, hw), "p5b", list(range(1, NTH)))
        final_res_th(y2m, list(range(1, NTH)), "p6b")
    em.barrier()

    em.finalize(nc)
    with nc.Block() as block:
        @block.sync
        def _(e):
            em.emit(e, "sync")

        @block.tensor
        def _(e):
            em.emit(e, "tensor")

        @block.vector
        def _(e):
            em.emit(e, "vector")

        @block.scalar
        def _(e):
            em.emit(e, "scalar")

        @block.gpsimd
        def _(e):
            em.emit(e, "gpsimd")
    return nc


def make_consts():
    tri = np.zeros((128, 4 * 512), dtype=ml_dtypes.bfloat16)
    for j in range(4):
        for p in range(128):
            tri[p, j * 512 + j * 128 + p:(j + 1) * 512] = 1.0
    ident = np.eye(128, dtype=ml_dtypes.bfloat16)
    iota = np.tile(np.arange(E, dtype=np.float32), (NSLOT, 1))
    expand = np.zeros((E, 128), dtype=ml_dtypes.bfloat16)
    for e in range(E):
        expand[e, e * R:(e + 1) * R] = SCALING
    return tri, ident, iota, expand


def make_in_maps(T, inputs):
    x = np.ascontiguousarray(np.asarray(inputs["L0_attn_hidden_states"],
                                        dtype=np.float32).reshape(T, D))
    wq = np.asarray(inputs["L0_attn_w_q"], dtype=np.float32)
    wk = np.asarray(inputs["L0_attn_w_k"], dtype=np.float32)
    wv = np.asarray(inputs["L0_attn_w_v"], dtype=np.float32)
    wo = np.asarray(inputs["L0_attn_w_o"], dtype=np.float32)
    bw = np.asarray(inputs["L0_ffn_base_W"], dtype=np.float32)
    bb = np.asarray(inputs["L0_ffn_base_b"], dtype=np.float32)
    rw = np.asarray(inputs["L0_ffn_router_W"], dtype=np.float32)
    rbv = np.asarray(inputs["L0_ffn_router_b"], dtype=np.float32)
    Am = np.asarray(inputs["L0_ffn_A"], dtype=np.float32)
    Bm = np.asarray(inputs["L0_ffn_B"], dtype=np.float32)
    emap = np.asarray(inputs["L0_ffn_expert_map"], dtype=np.int32)

    def tile4(w):
        return np.ascontiguousarray(
            w.reshape(32, 128, 32, 128).transpose(2, 1, 0, 3))
    bw_t = tile4(bw.T)
    bb_t = np.ascontiguousarray(bb.reshape(32, 128).T)
    rw_t = np.zeros((D, 16), dtype=np.float32)
    rw_t[:, :NSLOT] = rw.T
    a_t = np.ascontiguousarray(Am.reshape(E * R, D).T)
    b_t = np.ascontiguousarray(Bm.transpose(0, 2, 1).reshape(E * R, D))
    wo_t4 = tile4(wo)
    tri, identc, iota, expand = make_consts()
    TPC = T // NCORES

    NQCh = T // 512
    CHKh = [(0, NQCh // 2), (NQCh // 2, 3 * NQCh // 4),
            (3 * NQCh // 4, 7 * NQCh // 8), (7 * NQCh // 8, NQCh)]
    maps = []
    for c in range(NCORES):
        maps.append({
            "x_full": x,
            "x_own": np.ascontiguousarray(x[c * TPC:(c + 1) * TPC]),
            "wq_my": np.ascontiguousarray(wq[:, c * HPC * HD:(c + 1) * HPC * HD]),
            "wk_my": np.ascontiguousarray(wk[:, c * HD:(c + 1) * HD]),
            "wv_my": np.ascontiguousarray(wv[:, c * HD:(c + 1) * HD]),
            "wo": wo_t4,
            "bw_t": bw_t,
            "bb_t": bb_t,
            "rw_t": rw_t,
            "rb": np.ascontiguousarray(rbv.reshape(NSLOT, 1)),
            "a_t": a_t,
            "b_t": b_t,
            "emap_i": np.ascontiguousarray(emap.reshape(NSLOT, 1)),
            "tri_c": tri,
            "id_in": identc,
            "iota_c": iota,
            "expand_c": expand,
            "goff": goff_c(c, T, CHKh),
        })
    return maps


def goff_c(c, T, CHK):
    t_lo = c * (T // NCORES)
    out = np.zeros((NCORES, 1), dtype=np.int32)
    g0 = 0
    for (q0, q1) in CHK:
        r0, r1 = q0 * 512, q1 * 512
        if r0 <= t_lo < r1:
            nrow = r1 - r0
            for rr in range(NCORES):
                out[rr, 0] = g0 + rr * nrow + (t_lo - r0)
            break
        g0 += (r1 - r0) * NCORES
    return out


_CACHED = {}


def kernel(**inputs):
    x = np.asarray(inputs["L0_attn_hidden_states"])
    T = x.shape[1]
    if T not in _CACHED:
        _CACHED[T] = build_program(T)
    nc = _CACHED[T]
    maps = make_in_maps(T, inputs)
    res = run_bass_kernel_spmd(nc, maps, core_ids=list(range(NCORES)))
    out = np.concatenate([np.asarray(res.results[c]["out"]) for c in range(NCORES)],
                         axis=0)
    return out.reshape(1, T, D).astype(np.float32)


# version 13
# speedup vs baseline: 1.0949x; 1.0949x over previous
import math
import numpy as np
import ml_dtypes

import concourse.bass as bass
import concourse.mybir as mybir
from concourse.bass_utils import run_bass_kernel_spmd

F32 = mybir.dt.float32
BF16 = mybir.dt.bfloat16
I32 = mybir.dt.int32
AF = mybir.ActivationFunctionType
ALU = mybir.AluOpType
AX = mybir.AxisListType

NCORES = 8
D = 4096
NH = 32
HD = 128
HPC = NH // NCORES
E = 8
R = 16
NSLOT = 2 * E - 1
EPS = 1e-5
SCALING = 2.0

ENGS = ("sync", "tensor", "vector", "scalar", "gpsimd")


NDMA_SLOTS = 8


class _Op:
    __slots__ = ("eng", "fn", "kind", "deps", "gseq", "needed", "event",
                 "inc_amt", "clock_after", "tag", "qslot")

    def __init__(self, eng, fn, kind, gseq, tag):
        self.eng = eng
        self.fn = fn
        self.kind = kind
        self.deps = []
        self.gseq = gseq
        self.needed = False
        self.event = None
        self.inc_amt = 1
        self.clock_after = None
        self.tag = tag
        self.qslot = 0


def _ap_region(ap):
    try:
        t = ap.tensor
        name = t.name
        esz = mybir.dt.size(ap.dtype)
        pairs = [(int(s), int(c)) for s, c in ap.ap]
        off = int(ap.offset)
        if type(t).__name__ == "DRamTensorHandle":
            ext = sum(abs(s) * (c - 1) for s, c in pairs)
            return (name, off * esz, (off + ext + 1) * esz)
        rowlen_b = mybir.dt.size(t.dtype)
        for s in list(t.shape)[1:]:
            rowlen_b *= int(s)
        rowlen = rowlen_b // esz
        pidx = None
        for i, (s, c) in enumerate(pairs):
            if s == rowlen and c <= 128:
                pidx = i
                break
        if pidx is None:
            return (name, 0, rowlen_b)
        f_lo = (off % rowlen) * esz
        ext = sum(abs(s) * (c - 1) for i, (s, c) in enumerate(pairs) if i != pidx) * esz
        return (name, f_lo, f_lo + ext + esz)
    except Exception:
        try:
            return (ap.tensor.name, 0, 1 << 60)
        except Exception:
            return ("?", 0, 1 << 60)


class Em:
    def __init__(self):
        self.eng_ops = {e: [] for e in ENGS}
        self.gseq = 0
        self.wlog = {}
        self.rlog = {}
        self.frontier = {}
        self.dma_ring = {e: [None] * NDMA_SLOTS for e in ENGS}
        self.dma_ctr = {e: 0 for e in ENGS}
        self.last_cc = None

    def _dadd(self, deps, op):
        if op.kind == 'cc':
            key = ('cc',)
        elif op.kind == 'd':
            key = ('d', op.eng, op.qslot)
        else:
            key = ('c', op.eng)
        cur = deps.get(key)
        if cur is None or op.gseq > cur.gseq:
            deps[key] = op

    def op(self, eng, fn, r=(), w=(), kind='c', tag=""):
        o = _Op(eng, fn, kind, self.gseq, tag)
        self.gseq += 1
        deps = {}
        if kind == 'd':
            o.qslot = self.dma_ctr[eng] % NDMA_SLOTS
            self.dma_ctr[eng] += 1
            prev = self.dma_ring[eng][o.qslot]
            if prev is not None:
                self._dadd(deps, prev)
            self.dma_ring[eng][o.qslot] = o
        elif kind == 'cc':
            if self.last_cc is not None:
                self._dadd(deps, self.last_cc)
            self.last_cc = o
        regs_r = [_ap_region(a) for a in r]
        regs_w = [_ap_region(a) for a in w]
        for (n, lo, hi) in regs_r + regs_w:
            for (l2, h2, wop) in self.wlog.get(n, ()):
                if lo < h2 and l2 < hi and wop is not o:
                    self._dadd(deps, wop)
        for (n, lo, hi) in regs_w:
            for (l2, h2, rop) in self.rlog.get(n, ()):
                if lo < h2 and l2 < hi and rop is not o:
                    self._dadd(deps, rop)
        for fop in self.frontier.values():
            self._dadd(deps, fop)
        o.deps = list(deps.values())
        for d in o.deps:
            if d.kind == 'c' and kind == 'c' and d.eng == 'tensor' and eng == 'tensor':
                continue
            d.needed = True
        for (n, lo, hi) in regs_w:
            lst = self.wlog.setdefault(n, [])
            lst[:] = [e2 for e2 in lst if not (lo <= e2[0] and e2[1] <= hi)]
            lst.append((lo, hi, o))
            rl = self.rlog.get(n)
            if rl:
                rl[:] = [e2 for e2 in rl if not (lo <= e2[0] and e2[1] <= hi)]
        for (n, lo, hi) in regs_r:
            self.rlog.setdefault(n, []).append((lo, hi, o))
        self.eng_ops[eng].append(o)
        return o

    def barrier(self):
        f = {}
        for e in ENGS:
            seen = set()
            for o in reversed(self.eng_ops[e]):
                key = (o.kind, o.qslot) if o.kind == 'd' else (o.kind,)
                if key not in seen:
                    o.needed = True
                    f[(e,) + key] = o
                    seen.add(key)
                if len(seen) >= 2 + NDMA_SLOTS:
                    break
        self.frontier = f

    def finalize(self, nc):
        import contextlib
        self._es = contextlib.ExitStack()
        sems = {}
        counts = {}

        def get(name):
            if name not in sems:
                sems[name] = self._es.enter_context(nc.semaphore(name))
                counts[name] = 0
        for e in ENGS:
            for o in self.eng_ops[e]:
                if o.kind == 'd':
                    sn = f"sd_{e}_{o.qslot}"
                    get(sn)
                    counts[sn] += 16
                    o.event = (sn, counts[sn])
                    o.inc_amt = 16
                elif o.kind == 'cc':
                    sn = "scc"
                    get(sn)
                    counts[sn] += 1
                    o.event = (sn, counts[sn])
                    o.inc_amt = 1
                elif o.needed:
                    sn = f"se_{e}"
                    get(sn)
                    counts[sn] += 1
                    o.event = (sn, counts[sn])
                    o.inc_amt = 1
        fe = []
        for sn, cnt in counts.items():
            if (sn.startswith("sd_") or sn == "scc") and cnt > 0:
                fe.append((sn, cnt))
        for e in ENGS:
            for o in reversed(self.eng_ops[e]):
                if o.event is not None and o.kind == 'c':
                    fe.append(o.event)
                    break
        self.sems = sems
        self.final_events = fe

    def emit(self, eng_obj, ename):
        sems = self.sems
        clock = {}
        for o in self.eng_ops[ename]:
            waits = {}
            for d in o.deps:
                if (d.kind == 'c' and o.kind == 'c' and d.eng == 'tensor'
                        and ename == 'tensor'):
                    continue
                if d.event is None:
                    continue
                sn, v = d.event
                if clock.get(sn, 0) < v and waits.get(sn, (0, None))[0] < v:
                    waits[sn] = (v, d)
            merged = False
            for sn, (v, d) in waits.items():
                eng_obj.wait_ge(sems[sn], v)
                if clock.get(sn, 0) < v:
                    clock[sn] = v
                if d.clock_after:
                    merged = True
                    for k2, v2 in d.clock_after.items():
                        if clock.get(k2, 0) < v2:
                            clock[k2] = v2
            inst = o.fn(eng_obj)
            if o.event is not None:
                assert inst is not None, f"op {o.tag} has no instruction for its event"
                inst.then_inc(sems[o.event[0]], o.inc_amt)
            if waits or merged:
                o.clock_after = dict(clock)
            else:
                o.clock_after = clock.copy() if clock else None
        if ename == "sync":
            for sn, v in self.final_events:
                eng_obj.wait_ge(sems[sn], v)


def build_program(T):
    TPC = T // NCORES
    NG2 = T // 256
    NT128 = T // 128
    NQC = T // 512
    OT = TPC // 128
    NTH = TPC // 512
    assert TPC % 512 == 0

    nc = bass.Bass("TRN2", target_bir_lowering=False)

    x_full = nc.dram_tensor("x_full", [T, D], F32, kind="ExternalInput")
    x_own = nc.dram_tensor("x_own", [TPC, D], F32, kind="ExternalInput")
    wq_my = nc.dram_tensor("wq_my", [D, HPC * HD], F32, kind="ExternalInput")
    wk_my = nc.dram_tensor("wk_my", [D, HD], F32, kind="ExternalInput")
    wv_my = nc.dram_tensor("wv_my", [D, HD], F32, kind="ExternalInput")
    wo = nc.dram_tensor("wo", [32, 128, 32, 128], F32, kind="ExternalInput")
    bw_t = nc.dram_tensor("bw_t", [32, 128, 32, 128], F32, kind="ExternalInput")
    bb_t = nc.dram_tensor("bb_t", [128, 32], F32, kind="ExternalInput")
    rw_t = nc.dram_tensor("rw_t", [D, 16], F32, kind="ExternalInput")
    rb = nc.dram_tensor("rb", [NSLOT, 1], F32, kind="ExternalInput")
    a_t = nc.dram_tensor("a_t", [D, 128], F32, kind="ExternalInput")
    b_t = nc.dram_tensor("b_t", [128, D], F32, kind="ExternalInput")
    emap_i = nc.dram_tensor("emap_i", [NSLOT, 1], I32, kind="ExternalInput")
    tri_c = nc.dram_tensor("tri_c", [128, 4 * 512], BF16, kind="ExternalInput")
    id_in = nc.dram_tensor("id_in", [128, 128], BF16, kind="ExternalInput")
    iota_c = nc.dram_tensor("iota_c", [NSLOT, E], F32, kind="ExternalInput")
    expand_c = nc.dram_tensor("expand_c", [E, 128], BF16, kind="ExternalInput")
    goff_d = nc.dram_tensor("goff", [NCORES, 1], I32, kind="ExternalInput")

    out_d = nc.dram_tensor("out", [TPC, D], F32, kind="ExternalOutput")

    from concourse.replica_groups import maybe_share_collective_output_space
    RG = [list(range(NCORES))]
    sp = maybe_share_collective_output_space("AllGather", RG)
    stats_ctr = nc.dram_tensor("stats_ctr", [128, 16], F32, kind="Internal")
    stats_g = nc.dram_tensor("stats_g", [128 * NCORES, 16], F32, kind="Internal",
                             addr_space=sp)
    attnm_ctr = nc.dram_tensor("attnm_ctr", [T, HPC * HD], BF16, kind="Internal")
    attnm_g = nc.dram_tensor("attnm_g", [T * NCORES, HPC * HD], BF16, kind="Internal",
                             addr_space=sp)
    x2_d = nc.dram_tensor("x2_d", [TPC, D], F32, kind="Internal")

    em = Em()
    S = {}
    P = {}

    def sal(name, shape, dtype, side="left"):
        g = nc.sbuf_tensor(name, shape, dtype, side=side)
        h = g.__enter__()
        S[name] = (g, h)
        return h

    def sfree(*names):
        for name in names:
            g, _ = S.pop(name)
            g.__exit__(None, None, None)

    def pal(name, shape, dtype):
        g = nc.psum_tensor(name, shape, dtype)
        h = g.__enter__()
        P[name] = (g, h)
        return h

    def pfree(*names):
        for name in names:
            g, _ = P.pop(name)
            g.__exit__(None, None, None)

    def dma(eng, dst, src, tag=""):
        return em.op(eng, lambda e, dd=dst, ss=src: e.dma_start(dd, ss),
                     r=[src], w=[dst], kind='d', tag=tag)

    cop = em.op

    ident = sal("ident", [128, 128], BF16)
    statsg = sal("statsg", [128, NCORES * 16], F32)
    cols = sal("cols", [128, 32], F32)
    iota_s = sal("iota_s", [NSLOT, E], F32)
    expand_s = sal("expand_s", [E, 128], BF16)
    emap_s = sal("emap_s", [NSLOT, 1], I32)
    emap_f = sal("emap_f", [NSLOT, 1], F32)
    rb_s = sal("rb_s", [NSLOT, 1], F32)
    bb_s = sal("bb_s", [128, 32], F32)
    m8 = sal("m8", [NSLOT, E], BF16)
    goff_sb = sal("goff_sb", [NCORES, 1], I32)
    cols2 = sal("cols2", [128, 32], F32)
    qh = [sal(f"qh{h}", [128, T], BF16) for h in range(HPC)]
    kk = sal("kk", [128, T], BF16)
    vt = sal("vt", [128, NT128 * 129], BF16)

    dma("sync", ident[:, :], id_in[:, :], tag="ld_ident")
    dma("sync", iota_s[:, :], iota_c[:, :], tag="ld_iota")
    dma("sync", expand_s[:, :], expand_c[:, :], tag="ld_expand")
    dma("sync", emap_s[:, :], emap_i[:, :], tag="ld_emap")
    dma("sync", rb_s[:, :], rb[:, :], tag="ld_rb")
    dma("sync", bb_s[:, :], bb_t[:, :], tag="ld_bb")
    dma("sync", goff_sb[:, :], goff_d[:, :], tag="ld_goff")
    cop("vector", lambda v: v.tensor_copy(emap_f[:, :], emap_s[:, :]),
        r=[emap_s[:, :]], w=[emap_f[:, :]], tag="emap_cast")
    cop("vector", lambda v: v.tensor_scalar(m8[:, :], iota_s[:, :], emap_f[:, 0:1],
                                            None, ALU.is_equal),
        r=[iota_s[:, :], emap_f[:, 0:1]], w=[m8[:, :]], tag="onehot")
    epsc = cols[:, 30:31]
    cop("vector", lambda v: v.memset(epsc, EPS), w=[epsc], tag="eps")
    vt_ones = vt[:, :].rearrange("p (n o) -> p n o", o=129)[:, :, 128:129]
    cop("vector", lambda v, a=vt_ones: v.memset(a, 1.0), w=[vt[:, :]], tag="vt_ones")

    wqs = sal("wqs", [128, 32 * HPC * HD], BF16, side="right")
    wks = sal("wks", [128, 32 * HD], BF16, side="right")
    wvs = sal("wvs", [128, 32 * HD], BF16, side="right")
    ln1t = [sal("ln1t0", [128, 32 * 256], BF16, side="right")]
    xb2 = [sal(f"xb{i}", [128, 2 * D], BF16, side="right") for i in range(2)]
    xb = xb2[0]
    vf = sal("vf", [128, 256], BF16, side="right")
    stats = sal("stats", [128, 16], F32, side="right")

    dma("gpsimd", wqs[:, :].rearrange("p (dc o) -> p dc o", dc=32),
        wq_my[:, :].rearrange("(dc p) o -> p dc o", p=128), tag="ld_wq")
    dma("gpsimd", wks[:, :].rearrange("p (dc o) -> p dc o", dc=32),
        wk_my[:, :].rearrange("(dc p) o -> p dc o", p=128), tag="ld_wk")
    dma("gpsimd", wvs[:, :].rearrange("p (dc o) -> p dc o", dc=32),
        wv_my[:, :].rearrange("(dc p) o -> p dc o", p=128), tag="ld_wv")

    xt = xb2[1][:, :].bitcast(F32)
    sq1 = ln1t[0][:, :].bitcast(F32)
    cop("vector", lambda v: v.memset(stats[:, :], 0.0), w=[stats[:, :]], tag="stats0")
    for tt in range(OT):
        dma("sync", xt, x_own[tt * 128:(tt + 1) * 128, :], tag=f"p1x{tt}")
        c0, c1 = cols[:, 0:1], cols[:, 1:2]
        cop("vector", lambda v, c0=c0: v.reduce_sum(c0, xt, axis=AX.X),
            r=[xt], w=[c0], tag=f"p1s{tt}")
        cop("scalar", lambda s, c1=c1: s.activation(sq1, xt, AF.Square, accum_out=c1),
            r=[xt], w=[sq1, c1], tag=f"p1q{tt}")
        mcol = stats[:, tt * 2:tt * 2 + 1]
        rcol = stats[:, tt * 2 + 1:tt * 2 + 2]
        c2, c3, c4, c5 = cols[:, 2:3], cols[:, 3:4], cols[:, 4:5], cols[:, 5:6]
        cop("vector", lambda v, c0=c0, mcol=mcol: v.tensor_scalar(
            mcol, c0, 1.0 / D, None, ALU.mult), r=[c0], w=[mcol], tag=f"p1m{tt}")
        cop("vector", lambda v, c1=c1, c2=c2: v.tensor_scalar(
            c2, c1, 1.0 / D, None, ALU.mult), r=[c1], w=[c2], tag=f"p1a{tt}")
        cop("vector", lambda v, mcol=mcol, c3=c3: v.tensor_tensor(
            c3, mcol, mcol, ALU.mult), r=[mcol], w=[c3], tag=f"p1b{tt}")
        cop("vector", lambda v, c2=c2, c3=c3, c4=c4: v.tensor_tensor(
            c4, c2, c3, ALU.subtract), r=[c2, c3], w=[c4], tag=f"p1c{tt}")
        cop("scalar", lambda s, c4=c4, c5=c5: s.activation(c5, c4, AF.Sqrt, bias=epsc),
            r=[c4, epsc], w=[c5], tag=f"p1d{tt}")
        cop("vector", lambda v, c5=c5, rcol=rcol: v.reciprocal(rcol, c5),
            r=[c5], w=[rcol], tag=f"p1e{tt}")
    dma("sync", stats_ctr[:, :], stats[:, :], tag="stats_out")
    cop("gpsimd", lambda g: g.collective_compute(
        "AllGather", ALU.bypass, replica_groups=RG,
        ins=[stats_ctr.ap().opt()], outs=[stats_g.ap().opt()]),
        r=[stats_ctr[:, :]], w=[stats_g[:, :]], kind='cc', tag="ag_stats")
    dma("sync", statsg[:, :].rearrange("p (r s) -> p r s", r=NCORES),
        stats_g[:, :].rearrange("(r p) s -> p r s", p=128), tag="statsg_in")

    ptr = [pal(f"ptr{i}", [128, 1024], BF16) for i in range(2)]
    pqkv = [pal(f"pqkv{i}", [128, 256], F32) for i in range(6)]

    def statcols(g128):
        rr, tt = g128 // OT, g128 % OT
        base = rr * 16 + tt * 2
        return statsg[:, base:base + 1], statsg[:, base + 1:base + 2]

    def p2_load(g):
        xv = xb2[g % 2]
        dma("gpsimd", xv[:, :].rearrange("p (u d) -> p u d", u=2),
            x_full[g * 256:(g + 1) * 256, :].rearrange("(u p) d -> p u d", p=128),
            tag=f"p2x{g}")
        for u in range(2):
            mcol, rcol = statcols(g * 2 + u)
            seg = xv[:, u * D:(u + 1) * D]
            cop("vector", lambda v, seg=seg, mcol=mcol, rcol=rcol: v.tensor_scalar(
                seg, seg, mcol, rcol, ALU.subtract, ALU.mult),
                r=[seg, mcol, rcol], w=[seg], tag=f"p2a{g}_{u}")

    def p2_tr(g):
        xv = xb2[g % 2]
        lt = ln1t[0]
        for dcq in range(8):
            pb = ptr[dcq % 2]
            for j in range(4):
                dc = dcq * 4 + j
                for u in range(2):
                    src = xv[:, u * D + dc * 128: u * D + dc * 128 + 128]
                    dst = pb[:, (j * 2 + u) * 128:(j * 2 + u) * 128 + 128]
                    cop("tensor", lambda t, src=src, dst=dst: t.transpose(
                        dst, src, ident[:, :]),
                        r=[src, ident[:, :]], w=[dst], tag=f"p2t{g}_{dc}_{u}")
            cop("vector", lambda v, pb=pb, lt=lt, dcq=dcq: v.tensor_copy(
                lt[:, dcq * 1024:(dcq + 1) * 1024], pb[:, :]),
                r=[pb[:, :]], w=[lt[:, dcq * 1024:(dcq + 1) * 1024]],
                tag=f"p2e{g}_{dcq}")

    def p2_mm(g):
        lt = ln1t[0]
        for oc in range(6):
            pq = pqkv[oc]
            for dc in range(32):
                if oc < 4:
                    lhsT = wqs[:, dc * 512 + oc * 128: dc * 512 + oc * 128 + 128]
                elif oc == 4:
                    lhsT = wks[:, dc * 128:(dc + 1) * 128]
                else:
                    lhsT = wvs[:, dc * 128:(dc + 1) * 128]
                rhs = lt[:, dc * 256:(dc + 1) * 256]
                cop("tensor", lambda t, pq=pq, lhsT=lhsT, rhs=rhs, dc=dc: t.matmul(
                    pq[:, :], lhsT, rhs, start=(dc == 0), stop=(dc == 31)),
                    r=[lhsT, rhs, pq[:, :]], w=[pq[:, :]], tag=f"p2m{g}_{oc}_{dc}")
            if oc < 4:
                dst = qh[oc][:, g * 256:(g + 1) * 256]
                cop("scalar", lambda s, pq=pq, dst=dst: s.copy(dst, pq[:, :]),
                    r=[pq[:, :]], w=[dst], tag=f"p2q{g}_{oc}")
            elif oc == 4:
                dst = kk[:, g * 256:(g + 1) * 256]
                cop("scalar", lambda s, pq=pq, dst=dst: s.copy(dst, pq[:, :]),
                    r=[pq[:, :]], w=[dst], tag=f"p2k{g}")
            else:
                cop("scalar", lambda s, pq=pq: s.copy(vf[:, :], pq[:, :]),
                    r=[pq[:, :]], w=[vf[:, :]], tag=f"p2v{g}")
                pb = ptr[g % 2]
                for u in range(2):
                    src = vf[:, u * 128:(u + 1) * 128]
                    dst = pb[:, u * 128:(u + 1) * 128]
                    cop("tensor", lambda t, src=src, dst=dst: t.transpose(
                        dst, src, ident[:, :]),
                        r=[src, ident[:, :]], w=[dst], tag=f"p2vt{g}_{u}")
                vdst = vt[:, :].rearrange("p (n o) -> p n o", o=129)[:, 2 * g:2 * g + 2, 0:128]
                cop("scalar", lambda s, pb=pb, vdst=vdst: s.copy(
                    vdst, pb[:, 0:256].rearrange("p (n o) -> p n o", o=128)),
                    r=[pb[:, 0:256]], w=[vt[:, 2 * g * 129:(2 * g + 2) * 129]],
                    tag=f"p2ve{g}")

    for g in range(NG2):
        p2_load(g)
        if g > 0:
            p2_mm(g - 1)
        p2_tr(g)
    p2_mm(NG2 - 1)

    em.barrier()
    sfree("stats", "vf", "xb1", "xb0", "ln1t0", "wvs", "wks", "wqs")
    pfree("pqkv5", "pqkv4", "pqkv3", "pqkv2", "pqkv1", "pqkv0", "ptr1", "ptr0")

    tri_s = sal("tri_s", [128, 4 * 512], BF16, side="right")
    attnm = sal("attnm", [128, NT128 * HPC * HD], BF16, side="right")
    pt = [sal(f"pt{i}", [128, 1024], BF16, side="right") for i in range(3)]
    dma("sync", tri_s[:, :], tri_c[:, :], tag="ld_tri")

    psc = [pal(f"psc{i}", [128, 1024], F32) for i in range(2)]
    pav = [pal(f"pav{i}", [128, 129], F32) for i in range(4)]

    def qk_exp(h, qc, kch, nkch, rot):
        q = qh[h]
        kc = kch // 2
        diag = (kc == qc)
        ps = psc[rot % 2]
        ptile = pt[rot % 3]
        for j2 in range(2):
            j = (kch % 2) * 2 + j2 if False else j2
            kb = kch * 256 + j2 * 128
            cop("tensor", lambda t, ps=ps, j2=j2, kb=kb, q=q, qc=qc: t.matmul(
                ps[:, j2 * 512:(j2 + 1) * 512], kk[:, kb:kb + 128],
                q[:, qc * 512:(qc + 1) * 512], start=True, stop=True),
                r=[kk[:, kb:kb + 128], q[:, qc * 512:(qc + 1) * 512],
                   ps[:, j2 * 512:(j2 + 1) * 512]],
                w=[ps[:, j2 * 512:(j2 + 1) * 512]], tag=f"qk{h}_{qc}_{kch}_{j2}")
        cop("scalar", lambda s, ptile=ptile, ps=ps: s.activation(
            ptile[:, :], ps[:, :], AF.Exp, scale=1.0 / math.sqrt(HD)),
            r=[ps[:, :]], w=[ptile[:, :]], tag=f"ex{h}_{qc}_{kch}")
        if diag:
            for j2 in range(2):
                j = (kch % 2) * 2 + j2
                pv = ptile[:, j2 * 512:(j2 + 1) * 512]
                tv = tri_s[:, j * 512:(j + 1) * 512]
                cop("vector", lambda v, pv=pv, tv=tv: v.tensor_tensor(
                    pv, pv, tv, ALU.mult),
                    r=[pv, tv], w=[pv], tag=f"mk{h}_{qc}_{kch}_{j2}")

    def av(h, qc, kch, nkch, rot):
        ptile = pt[rot % 3]
        for j2 in range(2):
            gt = kch * 2 + j2
            rhs = vt[:, gt * 129:(gt + 1) * 129]
            for qs in range(4):
                pa = pav[qs]
                lhsT = ptile[:, j2 * 512 + qs * 128: j2 * 512 + qs * 128 + 128]
                st = (kch == 0 and j2 == 0)
                sp = (kch == nkch - 1 and j2 == 1)
                cop("tensor", lambda t, pa=pa, lhsT=lhsT, rhs=rhs,
                    st=st, sp=sp: t.matmul(
                    pa[:, 0:129], lhsT, rhs, start=st, stop=sp),
                    r=[lhsT, rhs, pa[:, 0:129]],
                    w=[pa[:, 0:129]], tag=f"av{h}_{qc}_{kch}_{j2}_{qs}")

    CHK = [(0, NQC // 2), (NQC // 2, 3 * NQC // 4),
           (3 * NQC // 4, 7 * NQC // 8), (7 * NQC // 8, NQC)]

    def ag_chunk(k):
        q0, q1 = CHK[k]
        r0, r1 = q0 * 512, q1 * 512
        nrow = r1 - r0
        ctr_ap = attnm_ctr[r0:r1, :].rearrange("(n p) f -> p n f", p=128)
        src_ap = attnm[:, r0 // 128 * (HPC * HD): r1 // 128 * (HPC * HD)].rearrange(
            "p (n f) -> p n f", f=HPC * HD)
        dma("sync", ctr_ap, src_ap, tag=f"attnm_out{k}")
        g0 = sum((CHK[j][1] - CHK[j][0]) * 512 * NCORES for j in range(k))
        cop("gpsimd", lambda g: g.collective_compute(
            "AllGather", ALU.bypass, replica_groups=RG,
            ins=[attnm_ctr[r0:r1, :].opt()],
            outs=[attnm_g[g0:g0 + nrow * NCORES, :].opt()]),
            r=[attnm_ctr[r0:r1, :]], w=[attnm_g[g0:g0 + nrow * NCORES, :]],
            kind='cc', tag=f"ag_attnm{k}")

    rot = [0]
    for qc in range(NQC):
        nkch = 2 * (qc + 1)
        for h in range(HPC):
            base = rot[0]
            for kch in range(nkch):
                qk_exp(h, qc, kch, nkch, base + kch)
                if kch >= 2:
                    av(h, qc, kch - 2, nkch, base + kch - 2)
            av(h, qc, nkch - 2, nkch, base + nkch - 2)
            av(h, qc, nkch - 1, nkch, base + nkch - 1)
            rot[0] += nkch
            for qs in range(4):
                pa = pav[qs]
                rc = cols[:, 8 + qs:9 + qs]
                cop("vector", lambda v, pa=pa, rc=rc: v.reciprocal(
                    rc, pa[:, 128:129]),
                    r=[pa[:, 128:129]], w=[rc], tag=f"lr{h}_{qc}_{qs}")
                gt = qc * 4 + qs
                dst = attnm[:, gt * (HPC * HD) + h * HD: gt * (HPC * HD) + h * HD + HD]
                cop("vector", lambda v, dst=dst, pa=pa, rc=rc: v.tensor_scalar(
                    dst, pa[:, 0:128], rc, None, ALU.mult),
                    r=[pa[:, 0:128], rc], w=[dst], tag=f"an{h}_{qc}_{qs}")
        for k in range(len(CHK)):
            if qc == CHK[k][1] - 1:
                ag_chunk(k)

    em.barrier()
    sfree("pt2", "pt1", "pt0", "attnm", "tri_s")
    pfree("pav3", "pav2", "pav1", "pav0", "psc1", "psc0")
    sfree("vt", "kk", "qh3", "qh2", "qh1", "qh0")

    att = sal("att", [128, 32 * TPC], BF16)
    atg = sal("atg", [128, OT * D], BF16, side="right")
    FW = HPC * HD

    _offs = {}

    def mk_setup(e):
        for rr in range(NCORES):
            _offs[rr] = e.bass.values_load(
                goff_sb[rr:rr + 1, 0:1], engines=[e.engine], min_val=0,
                max_val=T * NCORES - TPC, skip_runtime_bounds_check=True)
        return e.nop()
    em.op("sync", mk_setup, r=[goff_sb[:, :]], tag="atg_setup")
    for rr in range(NCORES):
        def mk(e, rr=rr):
            src = attnm_g[bass.ds(_offs[rr], TPC), :].rearrange("(n p) f -> p n f", p=128)
            dstv = atg[:, :].rearrange("p (n d) -> p n d", d=D)[
                :, :, rr * FW:(rr + 1) * FW]
            return e.dma_start(dstv, src)
        em.op("sync", mk, r=[attnm_g[:, :]], w=[atg[:, :]], kind='d', tag=f"atg{rr}")

    ptr2 = [pal(f"ptr2_{i}", [128, 1024], BF16) for i in range(2)]
    for fc in range(32):
        pb = ptr2[fc % 2]
        for tt in range(OT):
            src = atg[:, tt * D + fc * 128: tt * D + fc * 128 + 128]
            dst = pb[:, tt * 128:(tt + 1) * 128]
            cop("tensor", lambda t, src=src, dst=dst: t.transpose(dst, src, ident[:, :]),
                r=[src, ident[:, :]], w=[dst], tag=f"p4t{fc}_{tt}")
        cop("vector", lambda v, pb=pb, fc=fc: v.tensor_copy(
            att[:, fc * TPC:(fc + 1) * TPC], pb[:, 0:TPC]),
            r=[pb[:, 0:TPC]], w=[att[:, fc * TPC:(fc + 1) * TPC]], tag=f"p4e{fc}")

    em.barrier()
    sfree("atg")
    pfree("ptr2_1", "ptr2_0")

    y1m = sal("y1m", [128, OT * D], BF16, side="right")
    wos = [sal(f"wos{i}", [128, 32 * 128], BF16, side="right") for i in range(2)]
    ystg = [sal(f"ystg{i}", [128, 512], BF16, side="right") for i in range(2)]

    py = [pal(f"py{i}", [128, 512], F32) for i in range(2)]
    ptr3 = [pal(f"ptr3_{i}", [128, 1024], BF16) for i in range(2)]

    def proj_pass(w_src, rhs_buf, dst_m, bias_cols, lora, tagp):
        backlog = []

        def tail(oc, th, pq, i):
            ys = ystg[i % 2]
            if bias_cols is not None:
                bc = bias_cols[:, oc:oc + 1]
                cop("scalar", lambda s, pq=pq, bc=bc, ys=ys: s.add(ys[:, :], pq[:, :], bc),
                    r=[pq[:, :], bc], w=[ys[:, :]], tag=f"{tagp}v{oc}_{th}")
            else:
                cop("scalar", lambda s, pq=pq, ys=ys: s.copy(ys[:, :], pq[:, :]),
                    r=[pq[:, :]], w=[ys[:, :]], tag=f"{tagp}v{oc}_{th}")
            pb = ptr3[i % 2]
            for u in range(4):
                src = ys[:, u * 128:(u + 1) * 128]
                dst = pb[:, u * 128:(u + 1) * 128]
                cop("tensor", lambda t, src=src, dst=dst: t.transpose(dst, src, ident[:, :]),
                    r=[src, ident[:, :]], w=[dst], tag=f"{tagp}t{oc}_{th}_{u}")
            dstv = dst_m[:, :].rearrange("p (n d) -> p n d", d=D)[
                :, th * 4:(th + 1) * 4, oc * 128:(oc + 1) * 128]
            cop("vector", lambda v, pb=pb, dstv=dstv: v.tensor_copy(
                dstv, pb[:, 0:512].rearrange("p (n o) -> p n o", o=128)),
                r=[pb[:, 0:512]], w=[dst_m[:, th * 4 * D:(th + 1) * 4 * D]],
                tag=f"{tagp}w{oc}_{th}")

        i = 0
        for oc in range(32):
            wb = wos[oc % 2]
            dma("gpsimd", wb[:, :],
                w_src[oc].rearrange("p dc o -> p (dc o)"), tag=f"{tagp}l{oc}")
            for th in range(NTH):
                pq = py[i % 2]
                for dc in range(32):
                    lhsT = wb[:, dc * 128:(dc + 1) * 128]
                    rhs = rhs_buf[:, dc * TPC + th * 512: dc * TPC + th * 512 + 512]
                    cop("tensor", lambda t, pq=pq, lhsT=lhsT, rhs=rhs, dc=dc: t.matmul(
                        pq[:, :], lhsT, rhs, start=(dc == 0), stop=(dc == 31 and lora is None)),
                        r=[lhsT, rhs, pq[:, :]], w=[pq[:, :]], tag=f"{tagp}m{oc}_{th}_{dc}")
                if lora is not None:
                    bt_s, hw_s = lora
                    lhsT = bt_s[:, oc * 128:(oc + 1) * 128]
                    rhs = hw_s[:, th * 512:(th + 1) * 512]
                    cop("tensor", lambda t, pq=pq, lhsT=lhsT, rhs=rhs: t.matmul(
                        pq[:, :], lhsT, rhs, start=False, stop=True),
                        r=[lhsT, rhs, pq[:, :]], w=[pq[:, :]], tag=f"{tagp}lo{oc}_{th}")
                backlog.append((oc, th, pq, i))
                if len(backlog) > 1:
                    tail(*backlog.pop(0))
                i += 1
        while backlog:
            tail(*backlog.pop(0))

    proj_pass(wo, att, y1m, None, None, "op")

    em.barrier()
    sfree("ystg1", "ystg0", "wos1", "wos0")
    sfree("att")
    ln2t = sal("ln2t", [128, 32 * TPC], BF16)
    xt2 = [sal(f"xt2_{i}", [128, D], F32, side="right") for i in range(2)]
    t1f = sal("t1f", [128, D], F32, side="right")
    junk2a = [sal(f"junk2_{i}", [128, D], BF16, side="right") for i in range(2)]

    def ln_residual(ym, src_x, dst_x, ln_dst, tagp):
        def stages(tt):
            CS = cols if tt % 2 == 0 else cols2
            xv = xt2[tt % 2]
            junk2 = junk2a[tt % 2]
            yv = ym[:, tt * D:(tt + 1) * D]
            c0, c1 = CS[:, 0:1], CS[:, 1:2]
            cm, ca, cb, cv, cs_, cr, cmr = (CS[:, 16:17], CS[:, 17:18], CS[:, 18:19],
                                            CS[:, 19:20], CS[:, 20:21], CS[:, 21:22],
                                            CS[:, 22:23])
            c6, c7 = CS[:, 6:7], CS[:, 7:8]
            d0, d1, d2, d3, d4, d5 = (CS[:, 23:24], CS[:, 24:25], CS[:, 25:26],
                                      CS[:, 26:27], CS[:, 27:28], CS[:, 28:29])

            def s0():
                dma("sync", xv[:, :], src_x[tt * 128:(tt + 1) * 128, :], tag=f"{tagp}x{tt}")
                cop("vector", lambda v: v.reduce_sum(c0, yv, axis=AX.X),
                    r=[yv], w=[c0], tag=f"{tagp}s{tt}")
                cop("scalar", lambda s: s.activation(t1f[:, :], yv, AF.Square, accum_out=c1),
                    r=[yv], w=[t1f[:, :], c1], tag=f"{tagp}q{tt}")

            def s1():
                cop("vector", lambda v: v.tensor_scalar(cm, c0, 1.0 / D, None, ALU.mult),
                    r=[c0], w=[cm], tag=f"{tagp}m{tt}")
                cop("vector", lambda v: v.tensor_scalar(ca, c1, 1.0 / D, None, ALU.mult),
                    r=[c1], w=[ca], tag=f"{tagp}a{tt}")
                cop("vector", lambda v: v.tensor_tensor(cb, cm, cm, ALU.mult),
                    r=[cm], w=[cb], tag=f"{tagp}b{tt}")
                cop("vector", lambda v: v.tensor_tensor(cv, ca, cb, ALU.subtract),
                    r=[ca, cb], w=[cv], tag=f"{tagp}c{tt}")
                cop("scalar", lambda s: s.activation(cs_, cv, AF.Sqrt, bias=epsc),
                    r=[cv, epsc], w=[cs_], tag=f"{tagp}d{tt}")

            def s2():
                cop("vector", lambda v: v.reciprocal(cr, cs_),
                    r=[cs_], w=[cr], tag=f"{tagp}e{tt}")
                cop("vector", lambda v: v.tensor_tensor(cmr, cm, cr, ALU.mult),
                    r=[cm, cr], w=[cmr], tag=f"{tagp}f{tt}")

            def s3():
                cop("vector", lambda v: v.tensor_scalar(
                    t1f[:, :], xv[:, :], cmr, None, ALU.subtract),
                    r=[xv[:, :], cmr], w=[t1f[:, :]], tag=f"{tagp}g{tt}")
                cop("vector", lambda v: v.scalar_tensor_tensor(
                    xv[:, :], yv, cr, t1f[:, :], ALU.mult, ALU.add),
                    r=[yv, cr, t1f[:, :]], w=[xv[:, :]], tag=f"{tagp}h{tt}")
                dma("sync", dst_x[tt * 128:(tt + 1) * 128, :], xv[:, :], tag=f"{tagp}o{tt}")
                cop("vector", lambda v: v.reduce_sum(c6, xv[:, :], axis=AX.X),
                    r=[xv[:, :]], w=[c6], tag=f"{tagp}2s{tt}")
                cop("scalar", lambda s: s.activation(t1f[:, :], xv[:, :], AF.Square, accum_out=c7),
                    r=[xv[:, :]], w=[t1f[:, :], c7], tag=f"{tagp}2q{tt}")

            def s4():
                cop("vector", lambda v: v.tensor_scalar(d0, c6, 1.0 / D, None, ALU.mult),
                    r=[c6], w=[d0], tag=f"{tagp}2m{tt}")
                cop("vector", lambda v: v.tensor_scalar(d1, c7, 1.0 / D, None, ALU.mult),
                    r=[c7], w=[d1], tag=f"{tagp}2a{tt}")
                cop("vector", lambda v: v.tensor_tensor(d2, d0, d0, ALU.mult),
                    r=[d0], w=[d2], tag=f"{tagp}2b{tt}")
                cop("vector", lambda v: v.tensor_tensor(d3, d1, d2, ALU.subtract),
                    r=[d1, d2], w=[d3], tag=f"{tagp}2c{tt}")
                cop("scalar", lambda s: s.activation(d4, d3, AF.Sqrt, bias=epsc),
                    r=[d3, epsc], w=[d4], tag=f"{tagp}2d{tt}")

            def s5():
                cop("vector", lambda v: v.reciprocal(d5, d4),
                    r=[d4], w=[d5], tag=f"{tagp}2e{tt}")
                cop("vector", lambda v: v.tensor_scalar(
                    junk2[:, :], xv[:, :], d0, d5, ALU.subtract, ALU.mult),
                    r=[xv[:, :], d0, d5], w=[junk2[:, :]], tag=f"{tagp}2f{tt}")

            def s6():
                for bq in range(4):
                    pb = ptr3[bq % 2]
                    for u in range(8):
                        dc = bq * 8 + u
                        src2 = junk2[:, dc * 128:(dc + 1) * 128]
                        dst2 = pb[:, u * 128:(u + 1) * 128]
                        cop("tensor", lambda t, src2=src2, dst2=dst2: t.transpose(
                            dst2, src2, ident[:, :]),
                            r=[src2, ident[:, :]], w=[dst2], tag=f"{tagp}2t{tt}_{dc}")
                    dstv = ln_dst[:, :].rearrange("p (c t) -> p c t", t=TPC)[
                        :, bq * 8:(bq + 1) * 8, tt * 128:(tt + 1) * 128]
                    cop("vector", lambda v, pb=pb, dstv=dstv: v.tensor_copy(
                        dstv, pb[:, :].rearrange("p (n o) -> p n o", o=128)),
                        r=[pb[:, :]], w=[ln_dst[:, :]], tag=f"{tagp}2w{tt}_{bq}")
            return [s0, s1, s2, s3, s4, s5, s6]

        for base in range(0, OT, 2):
            a = stages(base)
            b = stages(base + 1) if base + 1 < OT else None
            for si in range(7):
                a[si]()
                if b is not None:
                    b[si]()

    ln_residual(y1m, x_own, x2_d, ln2t, "p4")

    em.barrier()
    sfree("junk2_1", "junk2_0", "t1f", "xt2_1", "xt2_0", "y1m")
    pfree("ptr3_1", "ptr3_0", "py1", "py0")

    y2m = sal("y2m", [128, OT * D], BF16, side="right")
    bts = sal("bts", [128, D], BF16, side="right")
    hw = sal("hw", [128, TPC], BF16, side="right")
    rt = sal("rt", [128, 32 * 16], BF16, side="right")
    at = sal("at", [128, 32 * 128], BF16, side="right")
    identf = sal("identf", [128, 128], F32, side="right")
    lg = sal("lg", [NSLOT, TPC], F32, side="right")
    lgt = sal("lgt", [128, OT * 16], F32, side="right")
    probs2 = [sal(f"probs{i}", [128, 16], F32, side="right") for i in range(2)]
    gcnt2 = [sal(f"gcnt{i}", [128, 16], F32, side="right") for i in range(2)]
    selw2 = [sal(f"selw{i}", [128, 16 * NSLOT], F32, side="right") for i in range(2)]
    sps = sal("sps", [NSLOT, TPC], BF16, side="right")
    w8 = sal("w8", [E, TPC], BF16, side="right")
    wer = sal("wer", [128, TPC], BF16, side="right")

    dma("gpsimd", rt[:, :].rearrange("p (dc o) -> p dc o", dc=32),
        rw_t[:, :].rearrange("(dc p) o -> p dc o", p=128), tag="ld_rt")
    dma("gpsimd", at[:, :].rearrange("p (dc o) -> p dc o", dc=32),
        a_t[:, :].rearrange("(dc p) o -> p dc o", p=128), tag="ld_at")
    dma("gpsimd", bts[:, :], b_t[:, :], tag="ld_bt")
    cop("vector", lambda v: v.tensor_copy(identf[:, :], ident[:, :]),
        r=[ident[:, :]], w=[identf[:, :]], tag="identf")

    plg = [pal(f"plg{i}", [NSLOT, 512], F32) for i in range(2)]
    ptr4 = [pal(f"ptr4_{i}", [128, 512], F32) for i in range(2)]

    for th in range(NTH):
        pq = plg[th % 2]
        for dc in range(32):
            lhsT = rt[:, dc * 16:dc * 16 + NSLOT]
            rhs = ln2t[:, dc * TPC + th * 512: dc * TPC + th * 512 + 512]
            cop("tensor", lambda t, pq=pq, lhsT=lhsT, rhs=rhs, dc=dc: t.matmul(
                pq[:, :], lhsT, rhs, start=(dc == 0), stop=(dc == 31)),
                r=[lhsT, rhs, pq[:, :]], w=[pq[:, :]], tag=f"p5r{th}_{dc}")
        dst = lg[:, th * 512:(th + 1) * 512]
        cop("scalar", lambda s, pq=pq, dst=dst: s.add(dst, pq[:, :], rb_s[:, 0:1]),
            r=[pq[:, :], rb_s[:, 0:1]], w=[dst], tag=f"p5re{th}")
    for tt in range(OT):
        pb = ptr4[tt % 2]
        src = lg[:, tt * 128:(tt + 1) * 128]
        cop("tensor", lambda t, pb=pb, src=src: t.transpose(
            pb[:, 0:NSLOT], src, identf[0:NSLOT, 0:NSLOT]),
            r=[src, identf[:, :]], w=[pb[:, 0:16]], tag=f"p5lt{tt}")
        dst = lgt[:, tt * 16:tt * 16 + NSLOT]
        cop("scalar", lambda s, pb=pb, dst=dst: s.copy(dst, pb[:, 0:NSLOT]),
            r=[pb[:, 0:16]], w=[dst], tag=f"p5le{tt}")
    def router_tile(tt):
        CS = cols if tt % 2 == 0 else cols2
        probs = probs2[tt % 2]
        gcnt = gcnt2[tt % 2]
        selw = selw2[tt % 2]
        lv = lgt[:, tt * 16:tt * 16 + NSLOT]
        c0, c1, c2, c3 = CS[:, 0:1], CS[:, 1:2], CS[:, 2:3], CS[:, 3:4]

        def s0():
            cop("vector", lambda v: v.reduce_max(c0, lv, axis=AX.X),
                r=[lv], w=[c0], tag=f"p5mx{tt}")
            cop("vector", lambda v: v.tensor_scalar(c1, c0, -1.0, None, ALU.mult),
                r=[c0], w=[c1], tag=f"p5ng{tt}")
            cop("scalar", lambda s: s.activation(probs[:, 0:NSLOT], lv, AF.Exp, bias=c1),
                r=[lv, c1], w=[probs[:, 0:NSLOT]], tag=f"p5ex{tt}")

        def s1():
            cop("vector", lambda v: v.reduce_sum(c2, probs[:, 0:NSLOT], axis=AX.X),
                r=[probs[:, 0:NSLOT]], w=[c2], tag=f"p5sm{tt}")
            cop("vector", lambda v: v.reciprocal(c3, c2),
                r=[c2], w=[c3], tag=f"p5rc{tt}")
            cop("vector", lambda v: v.tensor_scalar(
                probs[:, 0:NSLOT], probs[:, 0:NSLOT], c3, None, ALU.mult),
                r=[probs[:, 0:NSLOT], c3], w=[probs[:, 0:NSLOT]], tag=f"p5pn{tt}")

        def s2():
            for sl in range(NSLOT):
                pc = probs[:, sl:sl + 1]
                dst = selw[:, sl * 16:sl * 16 + NSLOT]
                cop("vector", lambda v, pc=pc, dst=dst: v.tensor_scalar(
                    dst, probs[:, 0:NSLOT], pc, None, ALU.is_lt),
                    r=[probs[:, 0:NSLOT], pc], w=[dst], tag=f"p5c{tt}_{sl}")

        def s3():
            stride = 1
            while stride < NSLOT:
                for a in range(0, NSLOT - stride, 2 * stride):
                    b = a + stride
                    da = selw[:, a * 16:a * 16 + NSLOT]
                    db = selw[:, b * 16:b * 16 + NSLOT]
                    cop("vector", lambda v, da=da, db=db: v.tensor_tensor(
                        da, da, db, ALU.add),
                        r=[da, db], w=[da], tag=f"p5t{tt}_{a}_{b}")
                stride *= 2
            cop("vector", lambda v: v.tensor_scalar(
                gcnt[:, 0:NSLOT], selw[:, 0:NSLOT], float(E) - 0.5, None, ALU.is_lt),
                r=[selw[:, 0:NSLOT]], w=[gcnt[:, 0:NSLOT]], tag=f"p5sl{tt}")
            cop("vector", lambda v: v.tensor_tensor(
                probs[:, 0:NSLOT], probs[:, 0:NSLOT], gcnt[:, 0:NSLOT], ALU.mult),
                r=[probs[:, 0:NSLOT], gcnt[:, 0:NSLOT]], w=[probs[:, 0:NSLOT]],
                tag=f"p5sp{tt}")

        def s4():
            pb = ptr4[tt % 2]
            cop("tensor", lambda t, pb=pb: t.transpose(
                pb[0:NSLOT, 0:128], probs[:, 0:NSLOT], identf[:, :]),
                r=[probs[:, 0:NSLOT], identf[:, :]], w=[pb[:, 0:128]], tag=f"p5st{tt}")
            dst = sps[:, tt * 128:(tt + 1) * 128]
            cop("scalar", lambda s, pb=pb, dst=dst: s.copy(dst, pb[0:NSLOT, 0:128]),
                r=[pb[:, 0:128]], w=[dst], tag=f"p5se{tt}")
        return [s0, s1, s2, s3, s4]

    for base in range(0, OT, 2):
        a = router_tile(base)
        b = router_tile(base + 1) if base + 1 < OT else None
        for si in range(5):
            a[si]()
            if b is not None:
                b[si]()

    pfree("ptr4_1", "ptr4_0", "plg1", "plg0")
    pw8 = pal("pw8", [E, 512], F32)
    pwer = pal("pwer", [128, 512], F32)
    ph = [pal(f"ph{i}", [128, 512], F32) for i in range(2)]

    for th in range(NTH):
        rhs = sps[:, th * 512:(th + 1) * 512]
        cop("tensor", lambda t, rhs=rhs: t.matmul(pw8[:, :], m8[:, :], rhs,
                                                  start=True, stop=True),
            r=[m8[:, :], rhs, pw8[:, :]], w=[pw8[:, :]], tag=f"p5w8{th}")
        dst = w8[:, th * 512:(th + 1) * 512]
        cop("scalar", lambda s, dst=dst: s.copy(dst, pw8[:, :]),
            r=[pw8[:, :]], w=[dst], tag=f"p5w8e{th}")
        rhs2 = w8[:, th * 512:(th + 1) * 512]
        cop("tensor", lambda t, rhs2=rhs2: t.matmul(pwer[:, :], expand_s[:, :], rhs2,
                                                    start=True, stop=True),
            r=[expand_s[:, :], rhs2, pwer[:, :]], w=[pwer[:, :]], tag=f"p5we{th}")
        dst2 = wer[:, th * 512:(th + 1) * 512]
        cop("scalar", lambda s, dst2=dst2: s.copy(dst2, pwer[:, :]),
            r=[pwer[:, :]], w=[dst2], tag=f"p5wee{th}")
        pq3 = ph[th % 2]
        for dc in range(32):
            lhsT = at[:, dc * 128:(dc + 1) * 128]
            rhs3 = ln2t[:, dc * TPC + th * 512: dc * TPC + th * 512 + 512]
            cop("tensor", lambda t, pq3=pq3, lhsT=lhsT, rhs3=rhs3, dc=dc: t.matmul(
                pq3[:, :], lhsT, rhs3, start=(dc == 0), stop=(dc == 31)),
                r=[lhsT, rhs3, pq3[:, :]], w=[pq3[:, :]], tag=f"p5h{th}_{dc}")
        dst3 = hw[:, th * 512:(th + 1) * 512]
        w_in = wer[:, th * 512:(th + 1) * 512]
        cop("vector", lambda v, pq3=pq3, dst3=dst3, w_in=w_in: v.tensor_tensor(
            dst3, pq3[:, :], w_in, ALU.mult),
            r=[pq3[:, :], w_in], w=[dst3], tag=f"p5hw{th}")

    em.barrier()
    pfree("ph1", "ph0", "pwer", "pw8")
    sfree("wer", "w8", "sps", "selw1", "selw0", "gcnt1", "gcnt0", "probs1",
          "probs0", "lgt", "lg", "identf", "at", "rt")
    py2 = [pal(f"py2_{i}", [128, 512], F32) for i in range(2)]
    ptr5 = [pal(f"ptr5_{i}", [128, 1024], BF16) for i in range(5)]
    wos2 = [sal(f"wos2_{i}", [128, 32 * 128], BF16, side="right") for i in range(2)]
    ystg2 = [sal(f"ystg2_{i}", [128, 512], BF16, side="right") for i in range(2)]
    xt3 = sal("xt3", [128, D], F32, side="right")
    t1g = sal("t1g", [128, D], F32, side="right")

    def proj_pass2(w_src, rhs_buf, dst_m, bias_cols, lora, tagp, th_list):
        backlog = []

        def tail(oc, th, pq, i):
            ys = ystg2[i % 2]
            bc = bias_cols[:, oc:oc + 1]
            cop("scalar", lambda s, pq=pq, bc=bc, ys=ys: s.add(ys[:, :], pq[:, :], bc),
                r=[pq[:, :], bc], w=[ys[:, :]], tag=f"{tagp}v{oc}_{th}")
            pb = ptr5[i % 5]
            for u in range(4):
                src = ys[:, u * 128:(u + 1) * 128]
                dst = pb[:, u * 128:(u + 1) * 128]
                cop("tensor", lambda t, src=src, dst=dst: t.transpose(dst, src, ident[:, :]),
                    r=[src, ident[:, :]], w=[dst], tag=f"{tagp}t{oc}_{th}_{u}")
            dstv = dst_m[:, :].rearrange("p (n d) -> p n d", d=D)[
                :, th * 4:(th + 1) * 4, oc * 128:(oc + 1) * 128]
            cop("vector", lambda v, pb=pb, dstv=dstv: v.tensor_copy(
                dstv, pb[:, 0:512].rearrange("p (n o) -> p n o", o=128)),
                r=[pb[:, 0:512]], w=[dst_m[:, th * 4 * D:(th + 1) * 4 * D]],
                tag=f"{tagp}w{oc}_{th}")

        i = 0
        for th in th_list:
            for oc in range(32):
                wb = wos2[i % 2]
                dma("gpsimd", wb[:, :],
                    w_src[oc].rearrange("p dc o -> p (dc o)"), tag=f"{tagp}l{th}_{oc}")
                pq = py2[i % 2]
                for dc in range(32):
                    lhsT = wb[:, dc * 128:(dc + 1) * 128]
                    rhs = rhs_buf[:, dc * TPC + th * 512: dc * TPC + th * 512 + 512]
                    cop("tensor", lambda t, pq=pq, lhsT=lhsT, rhs=rhs, dc=dc: t.matmul(
                        pq[:, :], lhsT, rhs, start=(dc == 0), stop=False),
                        r=[lhsT, rhs, pq[:, :]], w=[pq[:, :]], tag=f"{tagp}m{oc}_{th}_{dc}")
                bt_s, hw_s = lora
                lhsT = bt_s[:, oc * 128:(oc + 1) * 128]
                rhs = hw_s[:, th * 512:(th + 1) * 512]
                cop("tensor", lambda t, pq=pq, lhsT=lhsT, rhs=rhs: t.matmul(
                    pq[:, :], lhsT, rhs, start=False, stop=True),
                    r=[lhsT, rhs, pq[:, :]], w=[pq[:, :]], tag=f"{tagp}lo{oc}_{th}")
                backlog.append((oc, th, pq, i))
                if len(backlog) > 1:
                    tail(*backlog.pop(0))
                i += 1
        while backlog:
            tail(*backlog.pop(0))

    def final_res_th(ym, th_list, tagp):
        tts = [th * 4 + i for th in th_list for i in range(4)]
        for tt in tts:
            CS = cols if tt % 2 == 0 else cols2
            xv = xt3
            dma("sync", xv[:, :], x2_d[tt * 128:(tt + 1) * 128, :], tag=f"{tagp}x{tt}")
            yv = ym[:, tt * D:(tt + 1) * D]
            c0, c1 = CS[:, 0:1], CS[:, 1:2]
            cop("vector", lambda v, yv=yv, c0=c0: v.reduce_sum(c0, yv, axis=AX.X),
                r=[yv], w=[c0], tag=f"{tagp}s{tt}")
            cop("scalar", lambda s, yv=yv, c1=c1: s.activation(
                t1g[:, :], yv, AF.Square, accum_out=c1),
                r=[yv], w=[t1g[:, :], c1], tag=f"{tagp}q{tt}")
            cm, ca, cb, cv, cs_, cr, cmr = (CS[:, 16:17], CS[:, 17:18], CS[:, 18:19],
                                            CS[:, 19:20], CS[:, 20:21], CS[:, 21:22],
                                            CS[:, 22:23])
            cop("vector", lambda v, c0=c0, cm=cm: v.tensor_scalar(
                cm, c0, 1.0 / D, None, ALU.mult), r=[c0], w=[cm], tag=f"{tagp}m{tt}")
            cop("vector", lambda v, c1=c1, ca=ca: v.tensor_scalar(
                ca, c1, 1.0 / D, None, ALU.mult), r=[c1], w=[ca], tag=f"{tagp}a{tt}")
            cop("vector", lambda v, cm=cm, cb=cb: v.tensor_tensor(
                cb, cm, cm, ALU.mult), r=[cm], w=[cb], tag=f"{tagp}b{tt}")
            cop("vector", lambda v, ca=ca, cb=cb, cv=cv: v.tensor_tensor(
                cv, ca, cb, ALU.subtract), r=[ca, cb], w=[cv], tag=f"{tagp}c{tt}")
            cop("scalar", lambda s, cv=cv, cs_=cs_: s.activation(cs_, cv, AF.Sqrt, bias=epsc),
                r=[cv, epsc], w=[cs_], tag=f"{tagp}d{tt}")
            cop("vector", lambda v, cs_=cs_, cr=cr: v.reciprocal(cr, cs_),
                r=[cs_], w=[cr], tag=f"{tagp}e{tt}")
            cop("vector", lambda v, cm=cm, cr=cr, cmr=cmr: v.tensor_tensor(
                cmr, cm, cr, ALU.mult), r=[cm, cr], w=[cmr], tag=f"{tagp}f{tt}")
            cop("vector", lambda v, xv=xv, cmr=cmr: v.tensor_scalar(
                t1g[:, :], xv[:, :], cmr, None, ALU.subtract),
                r=[xv[:, :], cmr], w=[t1g[:, :]], tag=f"{tagp}g{tt}")
            cop("vector", lambda v, yv=yv, cr=cr, xv=xv: v.scalar_tensor_tensor(
                xv[:, :], yv, cr, t1g[:, :], ALU.mult, ALU.add),
                r=[yv, cr, t1g[:, :]], w=[xv[:, :]], tag=f"{tagp}h{tt}")
            dma("sync", out_d[tt * 128:(tt + 1) * 128, :], xv[:, :], tag=f"{tagp}o{tt}")

    proj_pass2(bw_t, ln2t, y2m, bb_s, (bts, hw), "p5a", [0])
    final_res_th(y2m, [0], "p6a")
    if NTH > 1:
        proj_pass2(bw_t, ln2t, y2m, bb_s, (bts, hw), "p5b", list(range(1, NTH)))
        final_res_th(y2m, list(range(1, NTH)), "p6b")
    em.barrier()

    em.finalize(nc)
    with nc.Block() as block:
        @block.sync
        def _(e):
            em.emit(e, "sync")

        @block.tensor
        def _(e):
            em.emit(e, "tensor")

        @block.vector
        def _(e):
            em.emit(e, "vector")

        @block.scalar
        def _(e):
            em.emit(e, "scalar")

        @block.gpsimd
        def _(e):
            em.emit(e, "gpsimd")
    return nc


def make_consts():
    tri = np.zeros((128, 4 * 512), dtype=ml_dtypes.bfloat16)
    for j in range(4):
        for p in range(128):
            tri[p, j * 512 + j * 128 + p:(j + 1) * 512] = 1.0
    ident = np.eye(128, dtype=ml_dtypes.bfloat16)
    iota = np.tile(np.arange(E, dtype=np.float32), (NSLOT, 1))
    expand = np.zeros((E, 128), dtype=ml_dtypes.bfloat16)
    for e in range(E):
        expand[e, e * R:(e + 1) * R] = SCALING
    return tri, ident, iota, expand


def make_in_maps(T, inputs):
    x = np.ascontiguousarray(np.asarray(inputs["L0_attn_hidden_states"],
                                        dtype=np.float32).reshape(T, D))
    wq = np.asarray(inputs["L0_attn_w_q"], dtype=np.float32)
    wk = np.asarray(inputs["L0_attn_w_k"], dtype=np.float32)
    wv = np.asarray(inputs["L0_attn_w_v"], dtype=np.float32)
    wo = np.asarray(inputs["L0_attn_w_o"], dtype=np.float32)
    bw = np.asarray(inputs["L0_ffn_base_W"], dtype=np.float32)
    bb = np.asarray(inputs["L0_ffn_base_b"], dtype=np.float32)
    rw = np.asarray(inputs["L0_ffn_router_W"], dtype=np.float32)
    rbv = np.asarray(inputs["L0_ffn_router_b"], dtype=np.float32)
    Am = np.asarray(inputs["L0_ffn_A"], dtype=np.float32)
    Bm = np.asarray(inputs["L0_ffn_B"], dtype=np.float32)
    emap = np.asarray(inputs["L0_ffn_expert_map"], dtype=np.int32)

    def tile4(w):
        return np.ascontiguousarray(
            w.reshape(32, 128, 32, 128).transpose(2, 1, 0, 3))
    bw_t = tile4(bw.T)
    bb_t = np.ascontiguousarray(bb.reshape(32, 128).T)
    rw_t = np.zeros((D, 16), dtype=np.float32)
    rw_t[:, :NSLOT] = rw.T
    a_t = np.ascontiguousarray(Am.reshape(E * R, D).T)
    b_t = np.ascontiguousarray(Bm.transpose(0, 2, 1).reshape(E * R, D))
    wo_t4 = tile4(wo)
    tri, identc, iota, expand = make_consts()
    TPC = T // NCORES

    NQCh = T // 512
    CHKh = [(0, NQCh // 2), (NQCh // 2, 3 * NQCh // 4),
            (3 * NQCh // 4, 7 * NQCh // 8), (7 * NQCh // 8, NQCh)]
    maps = []
    for c in range(NCORES):
        maps.append({
            "x_full": x,
            "x_own": np.ascontiguousarray(x[c * TPC:(c + 1) * TPC]),
            "wq_my": np.ascontiguousarray(wq[:, c * HPC * HD:(c + 1) * HPC * HD]),
            "wk_my": np.ascontiguousarray(wk[:, c * HD:(c + 1) * HD]),
            "wv_my": np.ascontiguousarray(wv[:, c * HD:(c + 1) * HD]),
            "wo": wo_t4,
            "bw_t": bw_t,
            "bb_t": bb_t,
            "rw_t": rw_t,
            "rb": np.ascontiguousarray(rbv.reshape(NSLOT, 1)),
            "a_t": a_t,
            "b_t": b_t,
            "emap_i": np.ascontiguousarray(emap.reshape(NSLOT, 1)),
            "tri_c": tri,
            "id_in": identc,
            "iota_c": iota,
            "expand_c": expand,
            "goff": goff_c(c, T, CHKh),
        })
    return maps


def goff_c(c, T, CHK):
    t_lo = c * (T // NCORES)
    out = np.zeros((NCORES, 1), dtype=np.int32)
    g0 = 0
    for (q0, q1) in CHK:
        r0, r1 = q0 * 512, q1 * 512
        if r0 <= t_lo < r1:
            nrow = r1 - r0
            for rr in range(NCORES):
                out[rr, 0] = g0 + rr * nrow + (t_lo - r0)
            break
        g0 += (r1 - r0) * NCORES
    return out


_CACHED = {}


def kernel(**inputs):
    x = np.asarray(inputs["L0_attn_hidden_states"])
    T = x.shape[1]
    if T not in _CACHED:
        _CACHED[T] = build_program(T)
    nc = _CACHED[T]
    maps = make_in_maps(T, inputs)
    res = run_bass_kernel_spmd(nc, maps, core_ids=list(range(NCORES)))
    out = np.concatenate([np.asarray(res.results[c]["out"]) for c in range(NCORES)],
                         axis=0)
    return out.reshape(1, T, D).astype(np.float32)


# version 14 (final)
# speedup vs baseline: 1.0962x; 1.0012x over previous
import math
import numpy as np
import ml_dtypes

import concourse.bass as bass
import concourse.mybir as mybir
from concourse.bass_utils import run_bass_kernel_spmd

F32 = mybir.dt.float32
BF16 = mybir.dt.bfloat16
I32 = mybir.dt.int32
AF = mybir.ActivationFunctionType
ALU = mybir.AluOpType
AX = mybir.AxisListType

NCORES = 8
D = 4096
NH = 32
HD = 128
HPC = NH // NCORES
E = 8
R = 16
NSLOT = 2 * E - 1
EPS = 1e-5
SCALING = 2.0

ENGS = ("sync", "tensor", "vector", "scalar", "gpsimd")


NDMA_SLOTS = 8


class _Op:
    __slots__ = ("eng", "fn", "kind", "deps", "gseq", "needed", "event",
                 "inc_amt", "clock_after", "tag", "qslot")

    def __init__(self, eng, fn, kind, gseq, tag):
        self.eng = eng
        self.fn = fn
        self.kind = kind
        self.deps = []
        self.gseq = gseq
        self.needed = False
        self.event = None
        self.inc_amt = 1
        self.clock_after = None
        self.tag = tag
        self.qslot = 0


def _ap_region(ap):
    try:
        t = ap.tensor
        name = t.name
        esz = mybir.dt.size(ap.dtype)
        pairs = [(int(s), int(c)) for s, c in ap.ap]
        off = int(ap.offset)
        if type(t).__name__ == "DRamTensorHandle":
            ext = sum(abs(s) * (c - 1) for s, c in pairs)
            return (name, off * esz, (off + ext + 1) * esz)
        rowlen_b = mybir.dt.size(t.dtype)
        for s in list(t.shape)[1:]:
            rowlen_b *= int(s)
        rowlen = rowlen_b // esz
        pidx = None
        for i, (s, c) in enumerate(pairs):
            if s == rowlen and c <= 128:
                pidx = i
                break
        if pidx is None:
            return (name, 0, rowlen_b)
        f_lo = (off % rowlen) * esz
        ext = sum(abs(s) * (c - 1) for i, (s, c) in enumerate(pairs) if i != pidx) * esz
        return (name, f_lo, f_lo + ext + esz)
    except Exception:
        try:
            return (ap.tensor.name, 0, 1 << 60)
        except Exception:
            return ("?", 0, 1 << 60)


class Em:
    def __init__(self):
        self.eng_ops = {e: [] for e in ENGS}
        self.gseq = 0
        self.wlog = {}
        self.rlog = {}
        self.frontier = {}
        self.dma_ring = {e: [None] * NDMA_SLOTS for e in ENGS}
        self.dma_ctr = {e: 0 for e in ENGS}
        self.last_cc = None

    def _dadd(self, deps, op):
        if op.kind == 'cc':
            key = ('cc',)
        elif op.kind == 'd':
            key = ('d', op.eng, op.qslot)
        else:
            key = ('c', op.eng)
        cur = deps.get(key)
        if cur is None or op.gseq > cur.gseq:
            deps[key] = op

    def op(self, eng, fn, r=(), w=(), kind='c', tag=""):
        o = _Op(eng, fn, kind, self.gseq, tag)
        self.gseq += 1
        deps = {}
        if kind == 'd':
            o.qslot = self.dma_ctr[eng] % NDMA_SLOTS
            self.dma_ctr[eng] += 1
            prev = self.dma_ring[eng][o.qslot]
            if prev is not None:
                self._dadd(deps, prev)
            self.dma_ring[eng][o.qslot] = o
        elif kind == 'cc':
            if self.last_cc is not None:
                self._dadd(deps, self.last_cc)
            self.last_cc = o
        regs_r = [_ap_region(a) for a in r]
        regs_w = [_ap_region(a) for a in w]
        for (n, lo, hi) in regs_r + regs_w:
            for (l2, h2, wop) in self.wlog.get(n, ()):
                if lo < h2 and l2 < hi and wop is not o:
                    self._dadd(deps, wop)
        for (n, lo, hi) in regs_w:
            for (l2, h2, rop) in self.rlog.get(n, ()):
                if lo < h2 and l2 < hi and rop is not o:
                    self._dadd(deps, rop)
        for fop in self.frontier.values():
            self._dadd(deps, fop)
        o.deps = list(deps.values())
        for d in o.deps:
            if d.kind == 'c' and kind == 'c' and d.eng == 'tensor' and eng == 'tensor':
                continue
            d.needed = True
        for (n, lo, hi) in regs_w:
            lst = self.wlog.setdefault(n, [])
            lst[:] = [e2 for e2 in lst if not (lo <= e2[0] and e2[1] <= hi)]
            lst.append((lo, hi, o))
            rl = self.rlog.get(n)
            if rl:
                rl[:] = [e2 for e2 in rl if not (lo <= e2[0] and e2[1] <= hi)]
        for (n, lo, hi) in regs_r:
            self.rlog.setdefault(n, []).append((lo, hi, o))
        self.eng_ops[eng].append(o)
        return o

    def barrier(self):
        f = {}
        for e in ENGS:
            seen = set()
            for o in reversed(self.eng_ops[e]):
                key = (o.kind, o.qslot) if o.kind == 'd' else (o.kind,)
                if key not in seen:
                    o.needed = True
                    f[(e,) + key] = o
                    seen.add(key)
                if len(seen) >= 2 + NDMA_SLOTS:
                    break
        self.frontier = f

    def finalize(self, nc):
        import contextlib
        self._es = contextlib.ExitStack()
        sems = {}
        counts = {}

        def get(name):
            if name not in sems:
                sems[name] = self._es.enter_context(nc.semaphore(name))
                counts[name] = 0
        for e in ENGS:
            for o in self.eng_ops[e]:
                if o.kind == 'd':
                    sn = f"sd_{e}_{o.qslot}"
                    get(sn)
                    counts[sn] += 16
                    o.event = (sn, counts[sn])
                    o.inc_amt = 16
                elif o.kind == 'cc':
                    sn = "scc"
                    get(sn)
                    counts[sn] += 1
                    o.event = (sn, counts[sn])
                    o.inc_amt = 1
                elif o.needed:
                    sn = f"se_{e}"
                    get(sn)
                    counts[sn] += 1
                    o.event = (sn, counts[sn])
                    o.inc_amt = 1
        fe = []
        for sn, cnt in counts.items():
            if (sn.startswith("sd_") or sn == "scc") and cnt > 0:
                fe.append((sn, cnt))
        for e in ENGS:
            for o in reversed(self.eng_ops[e]):
                if o.event is not None and o.kind == 'c':
                    fe.append(o.event)
                    break
        self.sems = sems
        self.final_events = fe

    def emit(self, eng_obj, ename):
        sems = self.sems
        clock = {}
        for o in self.eng_ops[ename]:
            waits = {}
            for d in o.deps:
                if (d.kind == 'c' and o.kind == 'c' and d.eng == 'tensor'
                        and ename == 'tensor'):
                    continue
                if d.event is None:
                    continue
                sn, v = d.event
                if clock.get(sn, 0) < v and waits.get(sn, (0, None))[0] < v:
                    waits[sn] = (v, d)
            merged = False
            for sn, (v, d) in waits.items():
                eng_obj.wait_ge(sems[sn], v)
                if clock.get(sn, 0) < v:
                    clock[sn] = v
                if d.clock_after:
                    merged = True
                    for k2, v2 in d.clock_after.items():
                        if clock.get(k2, 0) < v2:
                            clock[k2] = v2
            inst = o.fn(eng_obj)
            if o.event is not None:
                assert inst is not None, f"op {o.tag} has no instruction for its event"
                inst.then_inc(sems[o.event[0]], o.inc_amt)
            if waits or merged:
                o.clock_after = dict(clock)
            else:
                o.clock_after = clock.copy() if clock else None
        if ename == "sync":
            for sn, v in self.final_events:
                eng_obj.wait_ge(sems[sn], v)


def build_program(T):
    TPC = T // NCORES
    NG2 = T // 256
    NT128 = T // 128
    NQC = T // 512
    OT = TPC // 128
    NTH = TPC // 512
    assert TPC % 512 == 0

    nc = bass.Bass("TRN2", target_bir_lowering=False)

    x_full = nc.dram_tensor("x_full", [T, D], F32, kind="ExternalInput")
    x_own = nc.dram_tensor("x_own", [TPC, D], F32, kind="ExternalInput")
    wq_my = nc.dram_tensor("wq_my", [D, HPC * HD], F32, kind="ExternalInput")
    wk_my = nc.dram_tensor("wk_my", [D, HD], F32, kind="ExternalInput")
    wv_my = nc.dram_tensor("wv_my", [D, HD], F32, kind="ExternalInput")
    wo = nc.dram_tensor("wo", [32, 128, 32, 128], F32, kind="ExternalInput")
    bw_t = nc.dram_tensor("bw_t", [32, 128, 32, 128], F32, kind="ExternalInput")
    bb_t = nc.dram_tensor("bb_t", [128, 32], F32, kind="ExternalInput")
    rw_t = nc.dram_tensor("rw_t", [D, 16], F32, kind="ExternalInput")
    rb = nc.dram_tensor("rb", [NSLOT, 1], F32, kind="ExternalInput")
    a_t = nc.dram_tensor("a_t", [D, 128], F32, kind="ExternalInput")
    b_t = nc.dram_tensor("b_t", [128, D], F32, kind="ExternalInput")
    emap_i = nc.dram_tensor("emap_i", [NSLOT, 1], I32, kind="ExternalInput")
    tri_c = nc.dram_tensor("tri_c", [128, 4 * 512], BF16, kind="ExternalInput")
    id_in = nc.dram_tensor("id_in", [128, 128], BF16, kind="ExternalInput")
    iota_c = nc.dram_tensor("iota_c", [NSLOT, E], F32, kind="ExternalInput")
    expand_c = nc.dram_tensor("expand_c", [E, 128], BF16, kind="ExternalInput")
    goff_d = nc.dram_tensor("goff", [NCORES, 1], I32, kind="ExternalInput")

    out_d = nc.dram_tensor("out", [TPC, D], F32, kind="ExternalOutput")

    from concourse.replica_groups import maybe_share_collective_output_space
    RG = [list(range(NCORES))]
    sp = maybe_share_collective_output_space("AllGather", RG)
    stats_ctr = nc.dram_tensor("stats_ctr", [128, 16], F32, kind="Internal")
    stats_g = nc.dram_tensor("stats_g", [128 * NCORES, 16], F32, kind="Internal",
                             addr_space=sp)
    attnm_ctr = nc.dram_tensor("attnm_ctr", [T, HPC * HD], BF16, kind="Internal")
    attnm_g = nc.dram_tensor("attnm_g", [T * NCORES, HPC * HD], BF16, kind="Internal",
                             addr_space=sp)
    x2_d = nc.dram_tensor("x2_d", [TPC, D], F32, kind="Internal")

    em = Em()
    S = {}
    P = {}

    def sal(name, shape, dtype, side="left"):
        g = nc.sbuf_tensor(name, shape, dtype, side=side)
        h = g.__enter__()
        S[name] = (g, h)
        return h

    def sfree(*names):
        for name in names:
            g, _ = S.pop(name)
            g.__exit__(None, None, None)

    def pal(name, shape, dtype):
        g = nc.psum_tensor(name, shape, dtype)
        h = g.__enter__()
        P[name] = (g, h)
        return h

    def pfree(*names):
        for name in names:
            g, _ = P.pop(name)
            g.__exit__(None, None, None)

    def dma(eng, dst, src, tag=""):
        return em.op(eng, lambda e, dd=dst, ss=src: e.dma_start(dd, ss),
                     r=[src], w=[dst], kind='d', tag=tag)

    cop = em.op

    ident = sal("ident", [128, 128], BF16)
    statsg = sal("statsg", [128, NCORES * 16], F32)
    cols = sal("cols", [128, 32], F32)
    iota_s = sal("iota_s", [NSLOT, E], F32)
    expand_s = sal("expand_s", [E, 128], BF16)
    emap_s = sal("emap_s", [NSLOT, 1], I32)
    emap_f = sal("emap_f", [NSLOT, 1], F32)
    rb_s = sal("rb_s", [NSLOT, 1], F32)
    bb_s = sal("bb_s", [128, 32], F32)
    m8 = sal("m8", [NSLOT, E], BF16)
    goff_sb = sal("goff_sb", [NCORES, 1], I32)
    cols2 = sal("cols2", [128, 32], F32)
    qh = [sal(f"qh{h}", [128, T], BF16) for h in range(HPC)]
    kk = sal("kk", [128, T], BF16)
    vt = sal("vt", [128, NT128 * 129], BF16)

    dma("sync", ident[:, :], id_in[:, :], tag="ld_ident")
    dma("sync", iota_s[:, :], iota_c[:, :], tag="ld_iota")
    dma("sync", expand_s[:, :], expand_c[:, :], tag="ld_expand")
    dma("sync", emap_s[:, :], emap_i[:, :], tag="ld_emap")
    dma("sync", rb_s[:, :], rb[:, :], tag="ld_rb")
    dma("sync", bb_s[:, :], bb_t[:, :], tag="ld_bb")
    dma("sync", goff_sb[:, :], goff_d[:, :], tag="ld_goff")
    cop("vector", lambda v: v.tensor_copy(emap_f[:, :], emap_s[:, :]),
        r=[emap_s[:, :]], w=[emap_f[:, :]], tag="emap_cast")
    cop("vector", lambda v: v.tensor_scalar(m8[:, :], iota_s[:, :], emap_f[:, 0:1],
                                            None, ALU.is_equal),
        r=[iota_s[:, :], emap_f[:, 0:1]], w=[m8[:, :]], tag="onehot")
    epsc = cols[:, 30:31]
    cop("vector", lambda v: v.memset(epsc, EPS), w=[epsc], tag="eps")
    vt_ones = vt[:, :].rearrange("p (n o) -> p n o", o=129)[:, :, 128:129]
    cop("vector", lambda v, a=vt_ones: v.memset(a, 1.0), w=[vt[:, :]], tag="vt_ones")

    wqs = sal("wqs", [128, 32 * HPC * HD], BF16, side="right")
    wks = sal("wks", [128, 32 * HD], BF16, side="right")
    wvs = sal("wvs", [128, 32 * HD], BF16, side="right")
    ln1t = [sal("ln1t0", [128, 32 * 256], BF16, side="right")]
    xb2 = [sal(f"xb{i}", [128, 2 * D], BF16, side="right") for i in range(2)]
    xb = xb2[0]
    vf = sal("vf", [128, 256], BF16, side="right")
    stats = sal("stats", [128, 16], F32, side="right")

    dma("gpsimd", wqs[:, :].rearrange("p (dc o) -> p dc o", dc=32),
        wq_my[:, :].rearrange("(dc p) o -> p dc o", p=128), tag="ld_wq")
    dma("gpsimd", wks[:, :].rearrange("p (dc o) -> p dc o", dc=32),
        wk_my[:, :].rearrange("(dc p) o -> p dc o", p=128), tag="ld_wk")
    dma("gpsimd", wvs[:, :].rearrange("p (dc o) -> p dc o", dc=32),
        wv_my[:, :].rearrange("(dc p) o -> p dc o", p=128), tag="ld_wv")

    xt = xb2[1][:, :].bitcast(F32)
    sq1 = ln1t[0][:, :].bitcast(F32)
    cop("vector", lambda v: v.memset(stats[:, :], 0.0), w=[stats[:, :]], tag="stats0")
    for tt in range(OT):
        dma("sync", xt, x_own[tt * 128:(tt + 1) * 128, :], tag=f"p1x{tt}")
        c0, c1 = cols[:, 0:1], cols[:, 1:2]
        cop("vector", lambda v, c0=c0: v.reduce_sum(c0, xt, axis=AX.X),
            r=[xt], w=[c0], tag=f"p1s{tt}")
        cop("scalar", lambda s, c1=c1: s.activation(sq1, xt, AF.Square, accum_out=c1),
            r=[xt], w=[sq1, c1], tag=f"p1q{tt}")
        mcol = stats[:, tt * 2:tt * 2 + 1]
        rcol = stats[:, tt * 2 + 1:tt * 2 + 2]
        c2, c3, c4, c5 = cols[:, 2:3], cols[:, 3:4], cols[:, 4:5], cols[:, 5:6]
        cop("vector", lambda v, c0=c0, mcol=mcol: v.tensor_scalar(
            mcol, c0, 1.0 / D, None, ALU.mult), r=[c0], w=[mcol], tag=f"p1m{tt}")
        cop("vector", lambda v, c1=c1, c2=c2: v.tensor_scalar(
            c2, c1, 1.0 / D, None, ALU.mult), r=[c1], w=[c2], tag=f"p1a{tt}")
        cop("vector", lambda v, mcol=mcol, c3=c3: v.tensor_tensor(
            c3, mcol, mcol, ALU.mult), r=[mcol], w=[c3], tag=f"p1b{tt}")
        cop("vector", lambda v, c2=c2, c3=c3, c4=c4: v.tensor_tensor(
            c4, c2, c3, ALU.subtract), r=[c2, c3], w=[c4], tag=f"p1c{tt}")
        cop("scalar", lambda s, c4=c4, c5=c5: s.activation(c5, c4, AF.Sqrt, bias=epsc),
            r=[c4, epsc], w=[c5], tag=f"p1d{tt}")
        cop("vector", lambda v, c5=c5, rcol=rcol: v.reciprocal(rcol, c5),
            r=[c5], w=[rcol], tag=f"p1e{tt}")
    dma("sync", stats_ctr[:, :], stats[:, :], tag="stats_out")
    cop("gpsimd", lambda g: g.collective_compute(
        "AllGather", ALU.bypass, replica_groups=RG,
        ins=[stats_ctr.ap().opt()], outs=[stats_g.ap().opt()]),
        r=[stats_ctr[:, :]], w=[stats_g[:, :]], kind='cc', tag="ag_stats")
    dma("sync", statsg[:, :].rearrange("p (r s) -> p r s", r=NCORES),
        stats_g[:, :].rearrange("(r p) s -> p r s", p=128), tag="statsg_in")

    ptr = [pal(f"ptr{i}", [128, 1024], BF16) for i in range(2)]
    pqkv = [pal(f"pqkv{i}", [128, 256], F32) for i in range(6)]

    def statcols(g128):
        rr, tt = g128 // OT, g128 % OT
        base = rr * 16 + tt * 2
        return statsg[:, base:base + 1], statsg[:, base + 1:base + 2]

    def p2_load(g):
        xv = xb2[g % 2]
        dma("gpsimd", xv[:, :].rearrange("p (u d) -> p u d", u=2),
            x_full[g * 256:(g + 1) * 256, :].rearrange("(u p) d -> p u d", p=128),
            tag=f"p2x{g}")
        for u in range(2):
            mcol, rcol = statcols(g * 2 + u)
            seg = xv[:, u * D:(u + 1) * D]
            cop("vector", lambda v, seg=seg, mcol=mcol, rcol=rcol: v.tensor_scalar(
                seg, seg, mcol, rcol, ALU.subtract, ALU.mult),
                r=[seg, mcol, rcol], w=[seg], tag=f"p2a{g}_{u}")

    def p2_tr(g):
        xv = xb2[g % 2]
        lt = ln1t[0]
        for dcq in range(8):
            pb = ptr[dcq % 2]
            for j in range(4):
                dc = dcq * 4 + j
                for u in range(2):
                    src = xv[:, u * D + dc * 128: u * D + dc * 128 + 128]
                    dst = pb[:, (j * 2 + u) * 128:(j * 2 + u) * 128 + 128]
                    cop("tensor", lambda t, src=src, dst=dst: t.transpose(
                        dst, src, ident[:, :]),
                        r=[src, ident[:, :]], w=[dst], tag=f"p2t{g}_{dc}_{u}")
            cop("vector", lambda v, pb=pb, lt=lt, dcq=dcq: v.tensor_copy(
                lt[:, dcq * 1024:(dcq + 1) * 1024], pb[:, :]),
                r=[pb[:, :]], w=[lt[:, dcq * 1024:(dcq + 1) * 1024]],
                tag=f"p2e{g}_{dcq}")

    def p2_mm(g):
        lt = ln1t[0]
        for oc in range(6):
            pq = pqkv[oc]
            for dc in range(32):
                if oc < 4:
                    lhsT = wqs[:, dc * 512 + oc * 128: dc * 512 + oc * 128 + 128]
                elif oc == 4:
                    lhsT = wks[:, dc * 128:(dc + 1) * 128]
                else:
                    lhsT = wvs[:, dc * 128:(dc + 1) * 128]
                rhs = lt[:, dc * 256:(dc + 1) * 256]
                cop("tensor", lambda t, pq=pq, lhsT=lhsT, rhs=rhs, dc=dc: t.matmul(
                    pq[:, :], lhsT, rhs, start=(dc == 0), stop=(dc == 31)),
                    r=[lhsT, rhs, pq[:, :]], w=[pq[:, :]], tag=f"p2m{g}_{oc}_{dc}")
            if oc < 4:
                dst = qh[oc][:, g * 256:(g + 1) * 256]
                cop("scalar", lambda s, pq=pq, dst=dst: s.copy(dst, pq[:, :]),
                    r=[pq[:, :]], w=[dst], tag=f"p2q{g}_{oc}")
            elif oc == 4:
                dst = kk[:, g * 256:(g + 1) * 256]
                cop("scalar", lambda s, pq=pq, dst=dst: s.copy(dst, pq[:, :]),
                    r=[pq[:, :]], w=[dst], tag=f"p2k{g}")
            else:
                cop("scalar", lambda s, pq=pq: s.copy(vf[:, :], pq[:, :]),
                    r=[pq[:, :]], w=[vf[:, :]], tag=f"p2v{g}")
                pb = ptr[g % 2]
                for u in range(2):
                    src = vf[:, u * 128:(u + 1) * 128]
                    dst = pb[:, u * 128:(u + 1) * 128]
                    cop("tensor", lambda t, src=src, dst=dst: t.transpose(
                        dst, src, ident[:, :]),
                        r=[src, ident[:, :]], w=[dst], tag=f"p2vt{g}_{u}")
                vdst = vt[:, :].rearrange("p (n o) -> p n o", o=129)[:, 2 * g:2 * g + 2, 0:128]
                cop("scalar", lambda s, pb=pb, vdst=vdst: s.copy(
                    vdst, pb[:, 0:256].rearrange("p (n o) -> p n o", o=128)),
                    r=[pb[:, 0:256]], w=[vt[:, 2 * g * 129:(2 * g + 2) * 129]],
                    tag=f"p2ve{g}")

    for g in range(NG2):
        p2_load(g)
        if g > 0:
            p2_mm(g - 1)
        p2_tr(g)
    p2_mm(NG2 - 1)

    em.barrier()
    sfree("stats", "vf", "xb1", "xb0", "ln1t0", "wvs", "wks", "wqs")
    pfree("pqkv5", "pqkv4", "pqkv3", "pqkv2", "pqkv1", "pqkv0", "ptr1", "ptr0")

    tri_s = sal("tri_s", [128, 4 * 512], BF16, side="right")
    attnm = sal("attnm", [128, NT128 * HPC * HD], BF16, side="right")
    pt = [sal(f"pt{i}", [128, 1024], BF16, side="right") for i in range(3)]
    dma("sync", tri_s[:, :], tri_c[:, :], tag="ld_tri")

    psc = [pal(f"psc{i}", [128, 1024], F32) for i in range(2)]
    pav = [pal(f"pav{i}", [128, 129], F32) for i in range(4)]

    def qk_exp(h, qc, kch, nkch, rot):
        q = qh[h]
        kc = kch // 2
        diag = (kc == qc)
        ps = psc[rot % 2]
        ptile = pt[rot % 3]
        for j2 in range(2):
            j = (kch % 2) * 2 + j2 if False else j2
            kb = kch * 256 + j2 * 128
            cop("tensor", lambda t, ps=ps, j2=j2, kb=kb, q=q, qc=qc: t.matmul(
                ps[:, j2 * 512:(j2 + 1) * 512], kk[:, kb:kb + 128],
                q[:, qc * 512:(qc + 1) * 512], start=True, stop=True),
                r=[kk[:, kb:kb + 128], q[:, qc * 512:(qc + 1) * 512],
                   ps[:, j2 * 512:(j2 + 1) * 512]],
                w=[ps[:, j2 * 512:(j2 + 1) * 512]], tag=f"qk{h}_{qc}_{kch}_{j2}")
        cop("scalar", lambda s, ptile=ptile, ps=ps: s.activation(
            ptile[:, :], ps[:, :], AF.Exp, scale=1.0 / math.sqrt(HD)),
            r=[ps[:, :]], w=[ptile[:, :]], tag=f"ex{h}_{qc}_{kch}")
        if diag:
            for j2 in range(2):
                j = (kch % 2) * 2 + j2
                pv = ptile[:, j2 * 512:(j2 + 1) * 512]
                tv = tri_s[:, j * 512:(j + 1) * 512]
                cop("vector", lambda v, pv=pv, tv=tv: v.tensor_tensor(
                    pv, pv, tv, ALU.mult),
                    r=[pv, tv], w=[pv], tag=f"mk{h}_{qc}_{kch}_{j2}")

    def av(h, qc, kch, nkch, rot):
        ptile = pt[rot % 3]
        for j2 in range(2):
            gt = kch * 2 + j2
            rhs = vt[:, gt * 129:(gt + 1) * 129]
            for qs in range(4):
                pa = pav[qs]
                lhsT = ptile[:, j2 * 512 + qs * 128: j2 * 512 + qs * 128 + 128]
                st = (kch == 0 and j2 == 0)
                sp = (kch == nkch - 1 and j2 == 1)
                cop("tensor", lambda t, pa=pa, lhsT=lhsT, rhs=rhs,
                    st=st, sp=sp: t.matmul(
                    pa[:, 0:129], lhsT, rhs, start=st, stop=sp),
                    r=[lhsT, rhs, pa[:, 0:129]],
                    w=[pa[:, 0:129]], tag=f"av{h}_{qc}_{kch}_{j2}_{qs}")

    CHK = [(0, NQC // 2), (NQC // 2, 3 * NQC // 4),
           (3 * NQC // 4, 7 * NQC // 8), (7 * NQC // 8, NQC)]

    def ag_chunk(k):
        q0, q1 = CHK[k]
        r0, r1 = q0 * 512, q1 * 512
        nrow = r1 - r0
        ctr_ap = attnm_ctr[r0:r1, :].rearrange("(n p) f -> p n f", p=128)
        src_ap = attnm[:, r0 // 128 * (HPC * HD): r1 // 128 * (HPC * HD)].rearrange(
            "p (n f) -> p n f", f=HPC * HD)
        dma("sync", ctr_ap, src_ap, tag=f"attnm_out{k}")
        g0 = sum((CHK[j][1] - CHK[j][0]) * 512 * NCORES for j in range(k))
        cop("gpsimd", lambda g: g.collective_compute(
            "AllGather", ALU.bypass, replica_groups=RG,
            ins=[attnm_ctr[r0:r1, :].opt()],
            outs=[attnm_g[g0:g0 + nrow * NCORES, :].opt()]),
            r=[attnm_ctr[r0:r1, :]], w=[attnm_g[g0:g0 + nrow * NCORES, :]],
            kind='cc', tag=f"ag_attnm{k}")

    rot = [0]
    for qc in range(NQC):
        nkch = 2 * (qc + 1)
        for h in range(HPC):
            base = rot[0]
            for kch in range(nkch):
                qk_exp(h, qc, kch, nkch, base + kch)
                if kch >= 2:
                    av(h, qc, kch - 2, nkch, base + kch - 2)
            av(h, qc, nkch - 2, nkch, base + nkch - 2)
            av(h, qc, nkch - 1, nkch, base + nkch - 1)
            rot[0] += nkch
            for qs in range(4):
                pa = pav[qs]
                rc = cols[:, 8 + qs:9 + qs]
                cop("vector", lambda v, pa=pa, rc=rc: v.reciprocal(
                    rc, pa[:, 128:129]),
                    r=[pa[:, 128:129]], w=[rc], tag=f"lr{h}_{qc}_{qs}")
                gt = qc * 4 + qs
                dst = attnm[:, gt * (HPC * HD) + h * HD: gt * (HPC * HD) + h * HD + HD]
                cop("vector", lambda v, dst=dst, pa=pa, rc=rc: v.tensor_scalar(
                    dst, pa[:, 0:128], rc, None, ALU.mult),
                    r=[pa[:, 0:128], rc], w=[dst], tag=f"an{h}_{qc}_{qs}")
        for k in range(len(CHK)):
            if qc == CHK[k][1] - 1:
                ag_chunk(k)

    em.barrier()
    sfree("pt2", "pt1", "pt0", "attnm", "tri_s")
    pfree("pav3", "pav2", "pav1", "pav0", "psc1", "psc0")
    sfree("vt", "kk", "qh3", "qh2", "qh1", "qh0")

    att = sal("att", [128, 32 * TPC], BF16)
    atg = sal("atg", [128, OT * D], BF16, side="right")
    FW = HPC * HD

    _offs = {}

    def mk_setup(e):
        for rr in range(NCORES):
            _offs[rr] = e.bass.values_load(
                goff_sb[rr:rr + 1, 0:1], engines=[e.engine], min_val=0,
                max_val=T * NCORES - TPC, skip_runtime_bounds_check=True)
        return e.nop()
    em.op("sync", mk_setup, r=[goff_sb[:, :]], tag="atg_setup")
    for rr in range(NCORES):
        def mk(e, rr=rr):
            src = attnm_g[bass.ds(_offs[rr], TPC), :].rearrange("(n p) f -> p n f", p=128)
            dstv = atg[:, :].rearrange("p (n d) -> p n d", d=D)[
                :, :, rr * FW:(rr + 1) * FW]
            return e.dma_start(dstv, src)
        em.op("sync", mk, r=[attnm_g[:, :]], w=[atg[:, :]], kind='d', tag=f"atg{rr}")

    ptr2 = [pal(f"ptr2_{i}", [128, 1024], BF16) for i in range(2)]
    for fc in range(32):
        pb = ptr2[fc % 2]
        for tt in range(OT):
            src = atg[:, tt * D + fc * 128: tt * D + fc * 128 + 128]
            dst = pb[:, tt * 128:(tt + 1) * 128]
            cop("tensor", lambda t, src=src, dst=dst: t.transpose(dst, src, ident[:, :]),
                r=[src, ident[:, :]], w=[dst], tag=f"p4t{fc}_{tt}")
        cop("vector", lambda v, pb=pb, fc=fc: v.tensor_copy(
            att[:, fc * TPC:(fc + 1) * TPC], pb[:, 0:TPC]),
            r=[pb[:, 0:TPC]], w=[att[:, fc * TPC:(fc + 1) * TPC]], tag=f"p4e{fc}")

    em.barrier()
    sfree("atg")
    pfree("ptr2_1", "ptr2_0")

    y1m = sal("y1m", [128, OT * D], BF16, side="right")
    wos = [sal(f"wos{i}", [128, 32 * 128], BF16, side="right") for i in range(2)]
    ystg = [sal(f"ystg{i}", [128, 512], BF16, side="right") for i in range(2)]

    py = [pal(f"py{i}", [128, 512], F32) for i in range(2)]
    ptr3 = [pal(f"ptr3_{i}", [128, 1024], BF16) for i in range(2)]

    def proj_pass(w_src, rhs_buf, dst_m, bias_cols, lora, tagp):
        backlog = []

        def tail(oc, th, pq, i):
            ys = ystg[i % 2]
            if bias_cols is not None:
                bc = bias_cols[:, oc:oc + 1]
                cop("scalar", lambda s, pq=pq, bc=bc, ys=ys: s.add(ys[:, :], pq[:, :], bc),
                    r=[pq[:, :], bc], w=[ys[:, :]], tag=f"{tagp}v{oc}_{th}")
            else:
                cop("scalar", lambda s, pq=pq, ys=ys: s.copy(ys[:, :], pq[:, :]),
                    r=[pq[:, :]], w=[ys[:, :]], tag=f"{tagp}v{oc}_{th}")
            pb = ptr3[i % 2]
            for u in range(4):
                src = ys[:, u * 128:(u + 1) * 128]
                dst = pb[:, u * 128:(u + 1) * 128]
                cop("tensor", lambda t, src=src, dst=dst: t.transpose(dst, src, ident[:, :]),
                    r=[src, ident[:, :]], w=[dst], tag=f"{tagp}t{oc}_{th}_{u}")
            dstv = dst_m[:, :].rearrange("p (n d) -> p n d", d=D)[
                :, th * 4:(th + 1) * 4, oc * 128:(oc + 1) * 128]
            cop("vector", lambda v, pb=pb, dstv=dstv: v.tensor_copy(
                dstv, pb[:, 0:512].rearrange("p (n o) -> p n o", o=128)),
                r=[pb[:, 0:512]], w=[dst_m[:, th * 4 * D:(th + 1) * 4 * D]],
                tag=f"{tagp}w{oc}_{th}")

        i = 0
        for oc in range(32):
            wb = wos[oc % 2]
            dma("gpsimd", wb[:, :],
                w_src[oc].rearrange("p dc o -> p (dc o)"), tag=f"{tagp}l{oc}")
            for th in range(NTH):
                pq = py[i % 2]
                for dc in range(32):
                    lhsT = wb[:, dc * 128:(dc + 1) * 128]
                    rhs = rhs_buf[:, dc * TPC + th * 512: dc * TPC + th * 512 + 512]
                    cop("tensor", lambda t, pq=pq, lhsT=lhsT, rhs=rhs, dc=dc: t.matmul(
                        pq[:, :], lhsT, rhs, start=(dc == 0), stop=(dc == 31 and lora is None)),
                        r=[lhsT, rhs, pq[:, :]], w=[pq[:, :]], tag=f"{tagp}m{oc}_{th}_{dc}")
                if lora is not None:
                    bt_s, hw_s = lora
                    lhsT = bt_s[:, oc * 128:(oc + 1) * 128]
                    rhs = hw_s[:, th * 512:(th + 1) * 512]
                    cop("tensor", lambda t, pq=pq, lhsT=lhsT, rhs=rhs: t.matmul(
                        pq[:, :], lhsT, rhs, start=False, stop=True),
                        r=[lhsT, rhs, pq[:, :]], w=[pq[:, :]], tag=f"{tagp}lo{oc}_{th}")
                backlog.append((oc, th, pq, i))
                if len(backlog) > 1:
                    tail(*backlog.pop(0))
                i += 1
        while backlog:
            tail(*backlog.pop(0))

    proj_pass(wo, att, y1m, None, None, "op")

    em.barrier()
    sfree("ystg1", "ystg0", "wos1", "wos0")
    sfree("att")
    ln2t = sal("ln2t", [128, 32 * TPC], BF16)
    xt2 = [sal(f"xt2_{i}", [128, D], F32, side="right") for i in range(2)]
    t1f = sal("t1f", [128, D], F32, side="right")
    junk2a = [sal(f"junk2_{i}", [128, D], BF16, side="right") for i in range(2)]

    def ln_residual(ym, src_x, dst_x, ln_dst, tagp):
        def stages(tt):
            CS = cols if tt % 2 == 0 else cols2
            xv = xt2[tt % 2]
            junk2 = junk2a[tt % 2]
            yv = ym[:, tt * D:(tt + 1) * D]
            c0, c1 = CS[:, 0:1], CS[:, 1:2]
            cm, ca, cb, cv, cs_, cr, cmr = (CS[:, 16:17], CS[:, 17:18], CS[:, 18:19],
                                            CS[:, 19:20], CS[:, 20:21], CS[:, 21:22],
                                            CS[:, 22:23])
            c6, c7 = CS[:, 6:7], CS[:, 7:8]
            d0, d1, d2, d3, d4, d5 = (CS[:, 23:24], CS[:, 24:25], CS[:, 25:26],
                                      CS[:, 26:27], CS[:, 27:28], CS[:, 28:29])

            def s0():
                dma("sync", xv[:, :], src_x[tt * 128:(tt + 1) * 128, :], tag=f"{tagp}x{tt}")
                cop("vector", lambda v: v.reduce_sum(c0, yv, axis=AX.X),
                    r=[yv], w=[c0], tag=f"{tagp}s{tt}")
                cop("scalar", lambda s: s.activation(t1f[:, :], yv, AF.Square, accum_out=c1),
                    r=[yv], w=[t1f[:, :], c1], tag=f"{tagp}q{tt}")

            def s1():
                cop("vector", lambda v: v.tensor_scalar(cm, c0, 1.0 / D, None, ALU.mult),
                    r=[c0], w=[cm], tag=f"{tagp}m{tt}")
                cop("vector", lambda v: v.tensor_scalar(ca, c1, 1.0 / D, None, ALU.mult),
                    r=[c1], w=[ca], tag=f"{tagp}a{tt}")
                cop("vector", lambda v: v.tensor_tensor(cb, cm, cm, ALU.mult),
                    r=[cm], w=[cb], tag=f"{tagp}b{tt}")
                cop("vector", lambda v: v.tensor_tensor(cv, ca, cb, ALU.subtract),
                    r=[ca, cb], w=[cv], tag=f"{tagp}c{tt}")
                cop("scalar", lambda s: s.activation(cs_, cv, AF.Sqrt, bias=epsc),
                    r=[cv, epsc], w=[cs_], tag=f"{tagp}d{tt}")

            def s2():
                cop("vector", lambda v: v.reciprocal(cr, cs_),
                    r=[cs_], w=[cr], tag=f"{tagp}e{tt}")
                cop("vector", lambda v: v.tensor_tensor(cmr, cm, cr, ALU.mult),
                    r=[cm, cr], w=[cmr], tag=f"{tagp}f{tt}")

            def s3():
                cop("vector", lambda v: v.tensor_scalar(
                    t1f[:, :], xv[:, :], cmr, None, ALU.subtract),
                    r=[xv[:, :], cmr], w=[t1f[:, :]], tag=f"{tagp}g{tt}")
                cop("vector", lambda v: v.scalar_tensor_tensor(
                    xv[:, :], yv, cr, t1f[:, :], ALU.mult, ALU.add),
                    r=[yv, cr, t1f[:, :]], w=[xv[:, :]], tag=f"{tagp}h{tt}")
                dma("sync", dst_x[tt * 128:(tt + 1) * 128, :], xv[:, :], tag=f"{tagp}o{tt}")
                cop("vector", lambda v: v.reduce_sum(c6, xv[:, :], axis=AX.X),
                    r=[xv[:, :]], w=[c6], tag=f"{tagp}2s{tt}")
                cop("scalar", lambda s: s.activation(t1f[:, :], xv[:, :], AF.Square, accum_out=c7),
                    r=[xv[:, :]], w=[t1f[:, :], c7], tag=f"{tagp}2q{tt}")

            def s4():
                cop("vector", lambda v: v.tensor_scalar(d0, c6, 1.0 / D, None, ALU.mult),
                    r=[c6], w=[d0], tag=f"{tagp}2m{tt}")
                cop("vector", lambda v: v.tensor_scalar(d1, c7, 1.0 / D, None, ALU.mult),
                    r=[c7], w=[d1], tag=f"{tagp}2a{tt}")
                cop("vector", lambda v: v.tensor_tensor(d2, d0, d0, ALU.mult),
                    r=[d0], w=[d2], tag=f"{tagp}2b{tt}")
                cop("vector", lambda v: v.tensor_tensor(d3, d1, d2, ALU.subtract),
                    r=[d1, d2], w=[d3], tag=f"{tagp}2c{tt}")
                cop("scalar", lambda s: s.activation(d4, d3, AF.Sqrt, bias=epsc),
                    r=[d3, epsc], w=[d4], tag=f"{tagp}2d{tt}")

            def s5():
                cop("vector", lambda v: v.reciprocal(d5, d4),
                    r=[d4], w=[d5], tag=f"{tagp}2e{tt}")
                cop("vector", lambda v: v.tensor_scalar(
                    junk2[:, :], xv[:, :], d0, d5, ALU.subtract, ALU.mult),
                    r=[xv[:, :], d0, d5], w=[junk2[:, :]], tag=f"{tagp}2f{tt}")

            def s6():
                for bq in range(4):
                    pb = ptr3[bq % 2]
                    for u in range(8):
                        dc = bq * 8 + u
                        src2 = junk2[:, dc * 128:(dc + 1) * 128]
                        dst2 = pb[:, u * 128:(u + 1) * 128]
                        cop("tensor", lambda t, src2=src2, dst2=dst2: t.transpose(
                            dst2, src2, ident[:, :]),
                            r=[src2, ident[:, :]], w=[dst2], tag=f"{tagp}2t{tt}_{dc}")
                    dstv = ln_dst[:, :].rearrange("p (c t) -> p c t", t=TPC)[
                        :, bq * 8:(bq + 1) * 8, tt * 128:(tt + 1) * 128]
                    cop("vector", lambda v, pb=pb, dstv=dstv: v.tensor_copy(
                        dstv, pb[:, :].rearrange("p (n o) -> p n o", o=128)),
                        r=[pb[:, :]], w=[ln_dst[:, :]], tag=f"{tagp}2w{tt}_{bq}")
            return [s0, s1, s2, s3, s4, s5, s6]

        for base in range(0, OT, 2):
            a = stages(base)
            b = stages(base + 1) if base + 1 < OT else None
            for si in range(7):
                a[si]()
                if b is not None:
                    b[si]()

    ln_residual(y1m, x_own, x2_d, ln2t, "p4")

    em.barrier()
    sfree("junk2_1", "junk2_0", "t1f", "xt2_1", "xt2_0", "y1m")
    pfree("ptr3_1", "ptr3_0", "py1", "py0")

    y2m = sal("y2m", [128, OT * D], BF16, side="right")
    bts = sal("bts", [128, D], BF16, side="right")
    hw = sal("hw", [128, TPC], BF16, side="right")
    rt = sal("rt", [128, 32 * 16], BF16, side="right")
    at = sal("at", [128, 32 * 128], BF16, side="right")
    identf = sal("identf", [128, 128], F32, side="right")
    lg = sal("lg", [NSLOT, TPC], F32, side="right")
    lgt = sal("lgt", [128, OT * 16], F32, side="right")
    probs2 = [sal(f"probs{i}", [128, 16], F32, side="right") for i in range(2)]
    gcnt2 = [sal(f"gcnt{i}", [128, 16], F32, side="right") for i in range(2)]
    selw2 = [sal(f"selw{i}", [128, 16 * NSLOT], F32, side="right") for i in range(2)]
    sps = sal("sps", [NSLOT, TPC], BF16, side="right")
    w8 = sal("w8", [E, TPC], BF16, side="right")
    wer = sal("wer", [128, TPC], BF16, side="right")

    dma("gpsimd", rt[:, :].rearrange("p (dc o) -> p dc o", dc=32),
        rw_t[:, :].rearrange("(dc p) o -> p dc o", p=128), tag="ld_rt")
    dma("gpsimd", at[:, :].rearrange("p (dc o) -> p dc o", dc=32),
        a_t[:, :].rearrange("(dc p) o -> p dc o", p=128), tag="ld_at")
    dma("gpsimd", bts[:, :], b_t[:, :], tag="ld_bt")
    cop("vector", lambda v: v.tensor_copy(identf[:, :], ident[:, :]),
        r=[ident[:, :]], w=[identf[:, :]], tag="identf")

    plg = [pal(f"plg{i}", [NSLOT, 512], F32) for i in range(2)]
    ptr4 = [pal(f"ptr4_{i}", [128, 512], F32) for i in range(2)]

    for th in range(NTH):
        pq = plg[th % 2]
        for dc in range(32):
            lhsT = rt[:, dc * 16:dc * 16 + NSLOT]
            rhs = ln2t[:, dc * TPC + th * 512: dc * TPC + th * 512 + 512]
            cop("tensor", lambda t, pq=pq, lhsT=lhsT, rhs=rhs, dc=dc: t.matmul(
                pq[:, :], lhsT, rhs, start=(dc == 0), stop=(dc == 31)),
                r=[lhsT, rhs, pq[:, :]], w=[pq[:, :]], tag=f"p5r{th}_{dc}")
        dst = lg[:, th * 512:(th + 1) * 512]
        cop("scalar", lambda s, pq=pq, dst=dst: s.add(dst, pq[:, :], rb_s[:, 0:1]),
            r=[pq[:, :], rb_s[:, 0:1]], w=[dst], tag=f"p5re{th}")
    for tt in range(OT):
        pb = ptr4[tt % 2]
        src = lg[:, tt * 128:(tt + 1) * 128]
        cop("tensor", lambda t, pb=pb, src=src: t.transpose(
            pb[:, 0:NSLOT], src, identf[0:NSLOT, 0:NSLOT]),
            r=[src, identf[:, :]], w=[pb[:, 0:16]], tag=f"p5lt{tt}")
        dst = lgt[:, tt * 16:tt * 16 + NSLOT]
        cop("scalar", lambda s, pb=pb, dst=dst: s.copy(dst, pb[:, 0:NSLOT]),
            r=[pb[:, 0:16]], w=[dst], tag=f"p5le{tt}")
    def router_tile(tt):
        CS = cols if tt % 2 == 0 else cols2
        probs = probs2[tt % 2]
        gcnt = gcnt2[tt % 2]
        selw = selw2[tt % 2]
        lv = lgt[:, tt * 16:tt * 16 + NSLOT]
        c0, c1, c2, c3 = CS[:, 0:1], CS[:, 1:2], CS[:, 2:3], CS[:, 3:4]

        def s0():
            cop("vector", lambda v: v.reduce_max(c0, lv, axis=AX.X),
                r=[lv], w=[c0], tag=f"p5mx{tt}")
            cop("vector", lambda v: v.tensor_scalar(c1, c0, -1.0, None, ALU.mult),
                r=[c0], w=[c1], tag=f"p5ng{tt}")
            cop("scalar", lambda s: s.activation(probs[:, 0:NSLOT], lv, AF.Exp, bias=c1),
                r=[lv, c1], w=[probs[:, 0:NSLOT]], tag=f"p5ex{tt}")

        def s1():
            cop("vector", lambda v: v.reduce_sum(c2, probs[:, 0:NSLOT], axis=AX.X),
                r=[probs[:, 0:NSLOT]], w=[c2], tag=f"p5sm{tt}")
            cop("vector", lambda v: v.reciprocal(c3, c2),
                r=[c2], w=[c3], tag=f"p5rc{tt}")
            cop("vector", lambda v: v.tensor_scalar(
                probs[:, 0:NSLOT], probs[:, 0:NSLOT], c3, None, ALU.mult),
                r=[probs[:, 0:NSLOT], c3], w=[probs[:, 0:NSLOT]], tag=f"p5pn{tt}")

        def s2():
            for sl in range(NSLOT):
                pc = probs[:, sl:sl + 1]
                dst = selw[:, sl * 16:sl * 16 + NSLOT]
                cop("vector", lambda v, pc=pc, dst=dst: v.tensor_scalar(
                    dst, probs[:, 0:NSLOT], pc, None, ALU.is_lt),
                    r=[probs[:, 0:NSLOT], pc], w=[dst], tag=f"p5c{tt}_{sl}")

        def s3():
            stride = 1
            while stride < NSLOT:
                for a in range(0, NSLOT - stride, 2 * stride):
                    b = a + stride
                    da = selw[:, a * 16:a * 16 + NSLOT]
                    db = selw[:, b * 16:b * 16 + NSLOT]
                    cop("vector", lambda v, da=da, db=db: v.tensor_tensor(
                        da, da, db, ALU.add),
                        r=[da, db], w=[da], tag=f"p5t{tt}_{a}_{b}")
                stride *= 2
            cop("vector", lambda v: v.tensor_scalar(
                gcnt[:, 0:NSLOT], selw[:, 0:NSLOT], float(E) - 0.5, None, ALU.is_lt),
                r=[selw[:, 0:NSLOT]], w=[gcnt[:, 0:NSLOT]], tag=f"p5sl{tt}")
            cop("vector", lambda v: v.tensor_tensor(
                probs[:, 0:NSLOT], probs[:, 0:NSLOT], gcnt[:, 0:NSLOT], ALU.mult),
                r=[probs[:, 0:NSLOT], gcnt[:, 0:NSLOT]], w=[probs[:, 0:NSLOT]],
                tag=f"p5sp{tt}")

        def s4():
            pb = ptr4[tt % 2]
            cop("tensor", lambda t, pb=pb: t.transpose(
                pb[0:NSLOT, 0:128], probs[:, 0:NSLOT], identf[:, :]),
                r=[probs[:, 0:NSLOT], identf[:, :]], w=[pb[:, 0:128]], tag=f"p5st{tt}")
            dst = sps[:, tt * 128:(tt + 1) * 128]
            cop("scalar", lambda s, pb=pb, dst=dst: s.copy(dst, pb[0:NSLOT, 0:128]),
                r=[pb[:, 0:128]], w=[dst], tag=f"p5se{tt}")
        return [s0, s1, s2, s3, s4]

    for base in range(0, OT, 2):
        a = router_tile(base)
        b = router_tile(base + 1) if base + 1 < OT else None
        for si in range(5):
            a[si]()
            if b is not None:
                b[si]()

    pfree("ptr4_1", "ptr4_0", "plg1", "plg0")
    pw8 = pal("pw8", [E, 512], F32)
    pwer = pal("pwer", [128, 512], F32)
    ph = [pal(f"ph{i}", [128, 512], F32) for i in range(2)]

    for th in range(NTH):
        rhs = sps[:, th * 512:(th + 1) * 512]
        cop("tensor", lambda t, rhs=rhs: t.matmul(pw8[:, :], m8[:, :], rhs,
                                                  start=True, stop=True),
            r=[m8[:, :], rhs, pw8[:, :]], w=[pw8[:, :]], tag=f"p5w8{th}")
        dst = w8[:, th * 512:(th + 1) * 512]
        cop("scalar", lambda s, dst=dst: s.copy(dst, pw8[:, :]),
            r=[pw8[:, :]], w=[dst], tag=f"p5w8e{th}")
        rhs2 = w8[:, th * 512:(th + 1) * 512]
        cop("tensor", lambda t, rhs2=rhs2: t.matmul(pwer[:, :], expand_s[:, :], rhs2,
                                                    start=True, stop=True),
            r=[expand_s[:, :], rhs2, pwer[:, :]], w=[pwer[:, :]], tag=f"p5we{th}")
        dst2 = wer[:, th * 512:(th + 1) * 512]
        cop("scalar", lambda s, dst2=dst2: s.copy(dst2, pwer[:, :]),
            r=[pwer[:, :]], w=[dst2], tag=f"p5wee{th}")
        pq3 = ph[th % 2]
        for dc in range(32):
            lhsT = at[:, dc * 128:(dc + 1) * 128]
            rhs3 = ln2t[:, dc * TPC + th * 512: dc * TPC + th * 512 + 512]
            cop("tensor", lambda t, pq3=pq3, lhsT=lhsT, rhs3=rhs3, dc=dc: t.matmul(
                pq3[:, :], lhsT, rhs3, start=(dc == 0), stop=(dc == 31)),
                r=[lhsT, rhs3, pq3[:, :]], w=[pq3[:, :]], tag=f"p5h{th}_{dc}")
        dst3 = hw[:, th * 512:(th + 1) * 512]
        w_in = wer[:, th * 512:(th + 1) * 512]
        cop("vector", lambda v, pq3=pq3, dst3=dst3, w_in=w_in: v.tensor_tensor(
            dst3, pq3[:, :], w_in, ALU.mult),
            r=[pq3[:, :], w_in], w=[dst3], tag=f"p5hw{th}")

    em.barrier()
    pfree("ph1", "ph0", "pwer", "pw8")
    sfree("wer", "w8", "sps", "selw1", "selw0", "gcnt1", "gcnt0", "probs1",
          "probs0", "lgt", "lg", "identf", "at", "rt")
    py2 = [pal(f"py2_{i}", [128, 512], F32) for i in range(2)]
    ptr5 = [pal(f"ptr5_{i}", [128, 1024], BF16) for i in range(5)]
    wos2 = [sal(f"wos2_{i}", [128, 32 * 128], BF16, side="right") for i in range(2)]
    ystg2 = [sal(f"ystg2_{i}", [128, 512], BF16, side="right") for i in range(2)]
    xt3 = sal("xt3", [128, D], F32, side="right")
    t1g = sal("t1g", [128, D], F32, side="right")

    def proj_pass2(w_src, rhs_buf, dst_m, bias_cols, lora, tagp, th_list):
        backlog = []

        def tail(oc, th, pq, i):
            ys = ystg2[i % 2]
            bc = bias_cols[:, oc:oc + 1]
            cop("scalar", lambda s, pq=pq, bc=bc, ys=ys: s.add(ys[:, :], pq[:, :], bc),
                r=[pq[:, :], bc], w=[ys[:, :]], tag=f"{tagp}v{oc}_{th}")
            pb = ptr5[i % 5]
            for u in range(4):
                src = ys[:, u * 128:(u + 1) * 128]
                dst = pb[:, u * 128:(u + 1) * 128]
                cop("tensor", lambda t, src=src, dst=dst: t.transpose(dst, src, ident[:, :]),
                    r=[src, ident[:, :]], w=[dst], tag=f"{tagp}t{oc}_{th}_{u}")
            dstv = dst_m[:, :].rearrange("p (n d) -> p n d", d=D)[
                :, th * 4:(th + 1) * 4, oc * 128:(oc + 1) * 128]
            cop("scalar", lambda s, pb=pb, dstv=dstv: s.copy(
                dstv, pb[:, 0:512].rearrange("p (n o) -> p n o", o=128)),
                r=[pb[:, 0:512]], w=[dst_m[:, th * 4 * D:(th + 1) * 4 * D]],
                tag=f"{tagp}w{oc}_{th}")

        i = 0
        for th in th_list:
            for oc in range(32):
                wb = wos2[i % 2]
                dma("gpsimd", wb[:, :],
                    w_src[oc].rearrange("p dc o -> p (dc o)"), tag=f"{tagp}l{th}_{oc}")
                pq = py2[i % 2]
                for dc in range(32):
                    lhsT = wb[:, dc * 128:(dc + 1) * 128]
                    rhs = rhs_buf[:, dc * TPC + th * 512: dc * TPC + th * 512 + 512]
                    cop("tensor", lambda t, pq=pq, lhsT=lhsT, rhs=rhs, dc=dc: t.matmul(
                        pq[:, :], lhsT, rhs, start=(dc == 0), stop=False),
                        r=[lhsT, rhs, pq[:, :]], w=[pq[:, :]], tag=f"{tagp}m{oc}_{th}_{dc}")
                bt_s, hw_s = lora
                lhsT = bt_s[:, oc * 128:(oc + 1) * 128]
                rhs = hw_s[:, th * 512:(th + 1) * 512]
                cop("tensor", lambda t, pq=pq, lhsT=lhsT, rhs=rhs: t.matmul(
                    pq[:, :], lhsT, rhs, start=False, stop=True),
                    r=[lhsT, rhs, pq[:, :]], w=[pq[:, :]], tag=f"{tagp}lo{oc}_{th}")
                backlog.append((oc, th, pq, i))
                if len(backlog) > 1:
                    tail(*backlog.pop(0))
                i += 1
        while backlog:
            tail(*backlog.pop(0))

    def final_res_th(ym, th_list, tagp):
        tts = [th * 4 + i for th in th_list for i in range(4)]
        for tt in tts:
            CS = cols if tt % 2 == 0 else cols2
            xv = xt3
            dma("sync", xv[:, :], x2_d[tt * 128:(tt + 1) * 128, :], tag=f"{tagp}x{tt}")
            yv = ym[:, tt * D:(tt + 1) * D]
            c0, c1 = CS[:, 0:1], CS[:, 1:2]
            cop("vector", lambda v, yv=yv, c0=c0: v.reduce_sum(c0, yv, axis=AX.X),
                r=[yv], w=[c0], tag=f"{tagp}s{tt}")
            cop("scalar", lambda s, yv=yv, c1=c1: s.activation(
                t1g[:, :], yv, AF.Square, accum_out=c1),
                r=[yv], w=[t1g[:, :], c1], tag=f"{tagp}q{tt}")
            cm, ca, cb, cv, cs_, cr, cmr = (CS[:, 16:17], CS[:, 17:18], CS[:, 18:19],
                                            CS[:, 19:20], CS[:, 20:21], CS[:, 21:22],
                                            CS[:, 22:23])
            cop("vector", lambda v, c0=c0, cm=cm: v.tensor_scalar(
                cm, c0, 1.0 / D, None, ALU.mult), r=[c0], w=[cm], tag=f"{tagp}m{tt}")
            cop("vector", lambda v, c1=c1, ca=ca: v.tensor_scalar(
                ca, c1, 1.0 / D, None, ALU.mult), r=[c1], w=[ca], tag=f"{tagp}a{tt}")
            cop("vector", lambda v, cm=cm, cb=cb: v.tensor_tensor(
                cb, cm, cm, ALU.mult), r=[cm], w=[cb], tag=f"{tagp}b{tt}")
            cop("vector", lambda v, ca=ca, cb=cb, cv=cv: v.tensor_tensor(
                cv, ca, cb, ALU.subtract), r=[ca, cb], w=[cv], tag=f"{tagp}c{tt}")
            cop("scalar", lambda s, cv=cv, cs_=cs_: s.activation(cs_, cv, AF.Sqrt, bias=epsc),
                r=[cv, epsc], w=[cs_], tag=f"{tagp}d{tt}")
            cop("vector", lambda v, cs_=cs_, cr=cr: v.reciprocal(cr, cs_),
                r=[cs_], w=[cr], tag=f"{tagp}e{tt}")
            cop("vector", lambda v, cm=cm, cr=cr, cmr=cmr: v.tensor_tensor(
                cmr, cm, cr, ALU.mult), r=[cm, cr], w=[cmr], tag=f"{tagp}f{tt}")
            cop("vector", lambda v, xv=xv, cmr=cmr: v.tensor_scalar(
                t1g[:, :], xv[:, :], cmr, None, ALU.subtract),
                r=[xv[:, :], cmr], w=[t1g[:, :]], tag=f"{tagp}g{tt}")
            cop("vector", lambda v, yv=yv, cr=cr, xv=xv: v.scalar_tensor_tensor(
                xv[:, :], yv, cr, t1g[:, :], ALU.mult, ALU.add),
                r=[yv, cr, t1g[:, :]], w=[xv[:, :]], tag=f"{tagp}h{tt}")
            dma("sync", out_d[tt * 128:(tt + 1) * 128, :], xv[:, :], tag=f"{tagp}o{tt}")

    proj_pass2(bw_t, ln2t, y2m, bb_s, (bts, hw), "p5a", [0])
    final_res_th(y2m, [0], "p6a")
    if NTH > 1:
        proj_pass2(bw_t, ln2t, y2m, bb_s, (bts, hw), "p5b", list(range(1, NTH)))
        final_res_th(y2m, list(range(1, NTH)), "p6b")
    em.barrier()

    em.finalize(nc)
    with nc.Block() as block:
        @block.sync
        def _(e):
            em.emit(e, "sync")

        @block.tensor
        def _(e):
            em.emit(e, "tensor")

        @block.vector
        def _(e):
            em.emit(e, "vector")

        @block.scalar
        def _(e):
            em.emit(e, "scalar")

        @block.gpsimd
        def _(e):
            em.emit(e, "gpsimd")
    return nc


def make_consts():
    tri = np.zeros((128, 4 * 512), dtype=ml_dtypes.bfloat16)
    for j in range(4):
        for p in range(128):
            tri[p, j * 512 + j * 128 + p:(j + 1) * 512] = 1.0
    ident = np.eye(128, dtype=ml_dtypes.bfloat16)
    iota = np.tile(np.arange(E, dtype=np.float32), (NSLOT, 1))
    expand = np.zeros((E, 128), dtype=ml_dtypes.bfloat16)
    for e in range(E):
        expand[e, e * R:(e + 1) * R] = SCALING
    return tri, ident, iota, expand


def make_in_maps(T, inputs):
    x = np.ascontiguousarray(np.asarray(inputs["L0_attn_hidden_states"],
                                        dtype=np.float32).reshape(T, D))
    wq = np.asarray(inputs["L0_attn_w_q"], dtype=np.float32)
    wk = np.asarray(inputs["L0_attn_w_k"], dtype=np.float32)
    wv = np.asarray(inputs["L0_attn_w_v"], dtype=np.float32)
    wo = np.asarray(inputs["L0_attn_w_o"], dtype=np.float32)
    bw = np.asarray(inputs["L0_ffn_base_W"], dtype=np.float32)
    bb = np.asarray(inputs["L0_ffn_base_b"], dtype=np.float32)
    rw = np.asarray(inputs["L0_ffn_router_W"], dtype=np.float32)
    rbv = np.asarray(inputs["L0_ffn_router_b"], dtype=np.float32)
    Am = np.asarray(inputs["L0_ffn_A"], dtype=np.float32)
    Bm = np.asarray(inputs["L0_ffn_B"], dtype=np.float32)
    emap = np.asarray(inputs["L0_ffn_expert_map"], dtype=np.int32)

    def tile4(w):
        return np.ascontiguousarray(
            w.reshape(32, 128, 32, 128).transpose(2, 1, 0, 3))
    bw_t = tile4(bw.T)
    bb_t = np.ascontiguousarray(bb.reshape(32, 128).T)
    rw_t = np.zeros((D, 16), dtype=np.float32)
    rw_t[:, :NSLOT] = rw.T
    a_t = np.ascontiguousarray(Am.reshape(E * R, D).T)
    b_t = np.ascontiguousarray(Bm.transpose(0, 2, 1).reshape(E * R, D))
    wo_t4 = tile4(wo)
    tri, identc, iota, expand = make_consts()
    TPC = T // NCORES

    NQCh = T // 512
    CHKh = [(0, NQCh // 2), (NQCh // 2, 3 * NQCh // 4),
            (3 * NQCh // 4, 7 * NQCh // 8), (7 * NQCh // 8, NQCh)]
    maps = []
    for c in range(NCORES):
        maps.append({
            "x_full": x,
            "x_own": np.ascontiguousarray(x[c * TPC:(c + 1) * TPC]),
            "wq_my": np.ascontiguousarray(wq[:, c * HPC * HD:(c + 1) * HPC * HD]),
            "wk_my": np.ascontiguousarray(wk[:, c * HD:(c + 1) * HD]),
            "wv_my": np.ascontiguousarray(wv[:, c * HD:(c + 1) * HD]),
            "wo": wo_t4,
            "bw_t": bw_t,
            "bb_t": bb_t,
            "rw_t": rw_t,
            "rb": np.ascontiguousarray(rbv.reshape(NSLOT, 1)),
            "a_t": a_t,
            "b_t": b_t,
            "emap_i": np.ascontiguousarray(emap.reshape(NSLOT, 1)),
            "tri_c": tri,
            "id_in": identc,
            "iota_c": iota,
            "expand_c": expand,
            "goff": goff_c(c, T, CHKh),
        })
    return maps


def goff_c(c, T, CHK):
    t_lo = c * (T // NCORES)
    out = np.zeros((NCORES, 1), dtype=np.int32)
    g0 = 0
    for (q0, q1) in CHK:
        r0, r1 = q0 * 512, q1 * 512
        if r0 <= t_lo < r1:
            nrow = r1 - r0
            for rr in range(NCORES):
                out[rr, 0] = g0 + rr * nrow + (t_lo - r0)
            break
        g0 += (r1 - r0) * NCORES
    return out


_CACHED = {}


def kernel(**inputs):
    x = np.asarray(inputs["L0_attn_hidden_states"])
    T = x.shape[1]
    if T not in _CACHED:
        _CACHED[T] = build_program(T)
    nc = _CACHED[T]
    maps = make_in_maps(T, inputs)
    res = run_bass_kernel_spmd(nc, maps, core_ids=list(range(NCORES)))
    out = np.concatenate([np.asarray(res.results[c]["out"]) for c in range(NCORES)],
                         axis=0)
    return out.reshape(1, T, D).astype(np.float32)
